# version 02
# baseline (speedup 1.0000x reference)
import numpy as np
import ml_dtypes
import concourse.bass as bass
import concourse.mybir as mybir
from concourse.bass_utils import run_bass_kernel_spmd
from contextlib import ExitStack

F32 = mybir.dt.float32
BF16 = mybir.dt.bfloat16
AF = mybir.ActivationFunctionType
ALU = mybir.AluOpType
AX = mybir.AxisListType

MAGIC = float(np.float32(1.5 * 2 ** 23))
NEG_QMIN = -128.0
QMAX = 127.0


class Cfg:
    def __init__(self, T=2048, D=1024, H=16, I=2816, n_cores=8):
        self.T, self.D, self.H, self.I = T, D, H, I
        self.n_cores = n_cores
        self.DH = D // H
        self.NT = T // 128
        self.ND = D // 128
        self.NI = I // 128
        self.eps = 1e-5


class Sem:
    def __init__(self, nc, es, name):
        self.h = es.enter_context(nc.semaphore(name))
        self.n = 0

    def bump(self, k=1):
        self.n += k
        return self.n


class Env:
    ENGINES = ("sync", "tensor", "vector", "scalar", "gpsimd")

    def __init__(self, nc, es):
        self.nc = nc
        self.es = es
        self.ops = {e: [] for e in self.ENGINES}

    def sem(self, name):
        return Sem(self.nc, self.es, name)

    def emit(self, eng, f):
        self.ops[eng].append(f)

    def op(self, eng, meth, *, inc=None, **kw):
        if inc is not None:
            sem, k = inc
            sem.bump(k)
            tgt = (sem.h, k)
        else:
            tgt = None

        def f(e, meth=meth, kw=kw, tgt=tgt):
            ins = getattr(e, meth)(**kw)
            if tgt is not None:
                ins.then_inc(tgt[0], tgt[1])
        self.emit(eng, f)

    def wait(self, eng, sem, val=None):
        v = sem.n if val is None else val
        if v <= 0:
            return
        self.emit(eng, lambda e, h=sem.h, v=v: e.wait_ge(h, v))

    def wait_g(self, eng, guard):
        if guard is None:
            return
        sem, v = guard
        self.wait(eng, sem, v)

    def dma_dbg(self, eng, sem, *, out, in_):
        if sem.n > 0:
            self.wait(eng, sem, sem.n)
        sem.bump(16)
        tgt = sem.h

        def f(e, out=out, in_=in_, tgt=tgt):
            e.dma_start(out=out, in_=in_, allow_slow_non_contiguous=True).then_inc(tgt, 16)
        self.emit(eng, f)

    def dma(self, eng, sem, *, out, in_, throttle=True, accum=False):
        if throttle and sem.n > 0:
            self.wait(eng, sem, sem.n)
        sem.bump(16)
        tgt = sem.h

        def f(e, out=out, in_=in_, tgt=tgt, accum=accum):
            if accum:
                e.dma_start(out=out, in_=in_, accum_op=ALU.add).then_inc(tgt, 16)
            else:
                e.dma_start(out=out, in_=in_).then_inc(tgt, 16)
        self.emit(eng, f)

    def drain_inc(self, eng, sem=None, k=1):
        if sem is not None:
            sem.bump(k)
            self.emit(eng, lambda e, h=sem.h, k=k: e.drain().then_inc(h, k))
        else:
            self.emit(eng, lambda e: e.drain())

    def barrier(self, bar):
        base = bar.n
        for e in self.ENGINES:
            bar.bump(1)
            self.emit(e, lambda en, h=bar.h: en.sem_inc(h, 1))
        for e in self.ENGINES:
            self.wait(e, bar, base + len(self.ENGINES))

    def replay(self):
        with self.nc.Block() as block:
            for eng in self.ENGINES:
                if not self.ops[eng]:
                    continue
                deco = getattr(block, eng)

                def body(e, eng=eng):
                    for f in self.ops[eng]:
                        f(e)
                deco(body)


def build_consts(cfg):
    c = {}
    c["ident_f32"] = np.eye(128, dtype=np.float32)
    c["ident_bf16"] = np.eye(128, dtype=np.float32).astype(ml_dtypes.bfloat16)
    k = np.arange(128)[:, None]
    q = np.arange(128)[None, :]
    c["tri"] = (q >= k).astype(np.float32)
    DH = cfg.DH
    theta = (10000.0 ** (-2.0 * np.arange(DH // 2, dtype=np.float32) / DH)).astype(np.float32)
    pos = np.arange(cfg.T, dtype=np.float32)
    ang = (pos[:, None] * theta[None, :]).astype(np.float32)
    c["cos"] = np.cos(ang).astype(np.float32).reshape(cfg.NT, 128, DH // 2).transpose(1, 0, 2).copy()
    c["sin"] = np.sin(ang).astype(np.float32).reshape(cfg.NT, 128, DH // 2).transpose(1, 0, 2).copy()
    return c


class MK:
    def __init__(self, cfg, with_attn=True):
        self.cfg = cfg
        self.debug_taps = False
        self.with_attn = with_attn
        self.nc = bass.Bass("TRN2", target_bir_lowering=False, detect_race_conditions=False)
        self.es = ExitStack()
        self.env = Env(self.nc, self.es)
        self.ccin_dG = 0
        self._build()

    def sb(self, es, name, shape, dt=F32):
        return es.enter_context(self.nc.sbuf_tensor(name, shape, dt))

    def ps(self, es, name, shape):
        return es.enter_context(self.nc.psum_tensor(name, shape, F32))

    def _build(self):
        cfg, nc, env, es = self.cfg, self.nc, self.env, self.es
        T, D, H, I = cfg.T, cfg.D, cfg.H, cfg.I
        NT, ND, NI, DH = cfg.NT, cfg.ND, cfg.NI, cfg.DH

        self.x_in = nc.dram_tensor("x", [T, D], F32, kind="ExternalInput").ap()
        self.w = {}
        for nm, shp in [("wq", [D, D]), ("wk", [D, D]), ("wv", [D, D]), ("wo", [D, D]),
                        ("wg", [I, D]), ("wu", [I, D]), ("wd", [D, I])]:
            self.w[nm] = nc.dram_tensor(nm, shp, F32, kind="ExternalInput").ap()
        self.clipx = nc.dram_tensor("clipx", [1, 1], F32, kind="ExternalInput").ap()
        self.clipd = nc.dram_tensor("clipd", [1, 1], F32, kind="ExternalInput").ap()
        self.out = nc.dram_tensor("out", [T, D], F32, kind="ExternalOutput").ap()

        self.hpre_scr = nc.dram_tensor("hpre_scr", [I, T], F32, kind="Internal").ap()
        self.x1_scr = nc.dram_tensor("x1_scr", [T, D], F32, kind="Internal").ap()
        self.x2_scr = nc.dram_tensor("x2_scr", [T, D], F32, kind="Internal").ap()
        self.qr_scr = nc.dram_tensor("qr_scr", [D, T], F32, kind="Internal").ap()
        self.cc_in = nc.dram_tensor("cc_in", [1, 16], F32, kind="Internal")
        self.cc_out = nc.dram_tensor("cc_out", [1, 16], F32, kind="Internal")

        cst = build_consts(cfg)
        ident_f_d = nc.inline_tensor(cst["ident_f32"], name="ident_f").ap()
        ident_b_d = nc.inline_tensor(cst["ident_bf16"], name="ident_b").ap()
        tri_d = nc.inline_tensor(cst["tri"], name="tri_c").ap()
        cos_d = nc.inline_tensor(cst["cos"], name="cos_c").ap()
        sin_d = nc.inline_tensor(cst["sin"], name="sin_c").ap()

        s = self.sb
        self.identF = s(es, "identF", [128, 128])
        self.identB = s(es, "identB", [128, 128], BF16)
        self.tri = s(es, "tri", [128, 128])
        self.cC = s(es, "cC", [128, 1])
        self.ones_r = s(es, "ones_r", [1, 128])
        self.cc_sb = s(es, "cc_sb", [1, 16])
        self.gmax_bc = s(es, "gmax_bc", [128, 1])
        self.clipx_bc = s(es, "clipx_bc", [128, 1])
        self.clipd_bc = s(es, "clipd_bc", [128, 1])
        self.sig_t = s(es, "sig_t", [128, 1])
        self.scale_t = s(es, "scale_t", [128, 1])
        self.inv_bc = s(es, "inv_bc", [128, 1])
        self.sc127_bc = s(es, "sc127_bc", [128, 1])
        self.sums = s(es, "sums", [128, NT])
        self.sumsq = s(es, "sumsq", [128, NT])
        self.mean16 = s(es, "mean16", [128, NT])
        self.msq16 = s(es, "msq16", [128, NT])
        self.rstd = s(es, "rstd", [128, NT])
        self.negmr = s(es, "negmr", [128, NT])
        self.stA = s(es, "stA", [128, NT])
        self.stB = s(es, "stB", [128, NT])
        self.permax = s(es, "permax", [128, NT])
        self.lmax1 = s(es, "lmax1", [128, 1])
        self.rstd2 = s(es, "rstd2", [128, NT])
        self.negmr2 = s(es, "negmr2", [128, NT])

        e = env
        self.initS = e.sem("initS")
        self.ccD = e.sem("ccD")
        self.tP = e.sem("tP")
        self.vP = e.sem("vP")
        self.sP = e.sem("sP")
        self.ccS = e.sem("ccS")
        self.bar = e.sem("bar")

        self.cos_d, self.sin_d = cos_d, sin_d
        for dst, srct in [(self.identF, ident_f_d), (self.identB, ident_b_d), (self.tri, tri_d)]:
            e.dma("sync", self.initS, out=dst[:], in_=srct)
        e.dma("sync", self.initS, out=self.clipx_bc[:], in_=self.clipx.broadcast_to([128, 1]))
        e.dma("sync", self.initS, out=self.clipd_bc[:], in_=self.clipd.broadcast_to([128, 1]))
        self.eps_bc = s(es, "eps_bc", [128, 1])
        e.op("vector", "memset", ap=self.eps_bc[:], constant=self.cfg.eps)
        nc.const_aps.register(F32, self.cfg.eps, self.eps_bc[:])
        e.op("vector", "memset", ap=self.cC[:], constant=MAGIC)
        e.op("vector", "memset", ap=self.ones_r[:], constant=1.0)
        e.op("vector", "memset", ap=self.cc_sb[:], constant=0.0)
        e.drain_inc("vector", self.vP, 1)
        self.init_v = self.vP.n

        self.emit_ffn("f0", self.x_in,
                      write_out=self.x1_scr if self.with_attn else self.out,
                      ln2_stats=self.with_attn)
        if self.with_attn:
            self.emit_attn()
            self.emit_ffn("f1", self.x2_scr, write_out=self.out, ln2_stats=False)
        env.barrier(self.bar)
        env.replay()
        es.close()

    def _fold_and_allreduce(self, ps_small):
        e, tP, vP, ccD, ccS = self.env, self.tP, self.vP, self.ccD, self.ccS
        e.wait("tensor", vP, vP.n)
        e.wait("tensor", self.initS, self.initS.n)
        e.op("tensor", "transpose", out=ps_small[0:1, :], in_=self.lmax1[:],
             identity=self.identF[:], inc=(tP, 1))
        e.wait("vector", tP, tP.n)
        e.wait("vector", ccD, self.ccin_dG)
        e.wait("vector", self.vP, self.init_v)
        e.op("vector", "tensor_reduce", out=self.cc_sb[0:1, 0:1], in_=ps_small[0:1, :], axis=AX.X,
             op=ALU.max)
        e.drain_inc("vector", vP, 1)
        e.wait("gpsimd", vP, vP.n)
        e.dma("gpsimd", ccD, out=self.cc_in.ap(), in_=self.cc_sb[:])
        self.ccin_dG = ccD.n
        e.wait("gpsimd", ccD, ccD.n)
        e.op("gpsimd", "collective_compute",
             kind="AllReduce", op=ALU.max, replica_groups=[list(range(self.cfg.n_cores))],
             ins=[self.cc_in.ap().opt()], outs=[self.cc_out.ap().opt()], inc=(ccS, 1))
        e.wait("gpsimd", ccS, ccS.n)
        e.dma("gpsimd", ccD, out=self.gmax_bc[:],
              in_=self.cc_out.ap()[0:1, 0:1].broadcast_to([128, 1]))
        return ccD.n

    def _scale_path(self, clip_bc, g_gmax, extra_mul, ps_small):
        e, tP, vP, sP = self.env, self.tP, self.vP, self.sP
        e.wait("scalar", self.initS, self.initS.n)
        e.wait("scalar", vP, vP.n)
        e.op("scalar", "activation", out=self.sig_t[:], in_=clip_bc[:], func=AF.Sigmoid,
             inc=(sP, 1))
        e.wait("vector", sP, sP.n)
        e.wait("vector", self.ccD, g_gmax)
        e.op("vector", "scalar_tensor_tensor", out=self.scale_t[:], in0=self.sig_t[:],
             scalar=1.0 / QMAX, in1=self.gmax_bc[:], op0=ALU.mult, op1=ALU.mult)
        e.drain_inc("vector")
        e.op("vector", "reciprocal", out=self.inv_bc[:], in_=self.scale_t[:])
        e.op("vector", "tensor_scalar", out=self.sc127_bc[:], in0=self.scale_t[:],
             scalar1=float(extra_mul), scalar2=0.0, op0=ALU.mult, op1=ALU.bypass)
        e.drain_inc("vector", vP, 1)
        return vP.n

    def emit_ffn(self, ph, x_src, write_out, ln2_stats):
        cfg, nc, e = self.cfg, self.nc, self.env
        T, D, I, NT, ND, NI = cfg.T, cfg.D, cfg.I, cfg.NT, cfg.ND, cfg.NI
        st = {}
        tP, vP, sP = self.tP, self.vP, self.sP
        xld = e.sem(ph + "xld")
        wS = e.sem(ph + "wS")
        wdS = e.sem(ph + "wdS")
        hrS = e.sem(ph + "hrS")
        hpS = e.sem(ph + "hpS")
        xoS = e.sem(ph + "xoS")
        pes = ExitStack()
        s, p = self.sb, self.ps

        qwdT = s(pes, ph + "qwdT", [128, NI, D], BF16)
        ln_b = [s(pes, ph + f"ln{i}", [128, D]) for i in range(2)]
        qb_b = [s(pes, ph + f"qb{i}", [128, max(D, I)], BF16) for i in range(2)]
        xw_b = [s(pes, ph + f"xw{i}", [128, D]) for i in range(2)]
        sq_scr = s(pes, ph + "sq_scr", [128, D])
        rowvec = s(pes, ph + "rowvec", [1, D])
        ROWSCALE = s(pes, ph + "ROWSCALE", [128, D])
        ps_small = p(pes, ph + "ps_small", [128, 128])
        ps_tp = [es2.enter_context(nc.psum_tensor(ph + f"ps_tp{i}", [128, 128], BF16))
                 for i, es2 in ((0, pes), (1, pes))]
        tp_guard = [None, None]

        xr = x_src.rearrange("(tt p) d -> p tt d", p=128)

        def ln_stat_finish(rstd_t, negmr_t):
            e.drain_inc("vector")
            e.op("vector", "tensor_scalar", out=self.mean16[:], in0=self.sums[:],
                 scalar1=1.0 / D, scalar2=0.0, op0=ALU.mult, op1=ALU.bypass)
            e.op("vector", "tensor_scalar", out=self.sumsq[:], in0=self.sumsq[:],
                 scalar1=1.0 / D, scalar2=0.0, op0=ALU.mult, op1=ALU.bypass)
            e.drain_inc("vector")
            e.op("vector", "tensor_tensor", out=self.msq16[:], in0=self.mean16[:],
                 in1=self.mean16[:], op=ALU.mult)
            e.drain_inc("vector")
            e.op("vector", "tensor_tensor", out=self.sumsq[:], in0=self.sumsq[:],
                 in1=self.msq16[:], op=ALU.subtract)
            e.drain_inc("vector", vP, 1)
            e.wait("scalar", vP, vP.n)
            e.op("scalar", "activation", out=self.stA[:], in_=self.sumsq[:],
                 func=AF.Sqrt, bias=self.cfg.eps, inc=(sP, 1))
            e.wait("vector", sP, sP.n)
            e.op("vector", "reciprocal", out=rstd_t[:], in_=self.stA[:])
            e.drain_inc("vector")
            e.op("vector", "scalar_tensor_tensor", out=negmr_t[:], in0=self.mean16[:],
                 scalar=-1.0, in1=rstd_t[:], op0=ALU.mult, op1=ALU.mult)
            e.drain_inc("vector", vP, 1)
            return vP.n

        for tt in range(NT):
            b = tt % 2
            if tt >= 2:
                e.wait("sync", vP, st["st_red%d" % (tt - 2)])
            e.dma("sync", xld, out=xw_b[b][:], in_=xr[:, tt, :])
            e.wait("scalar", xld, xld.n)
            if tt >= 2:
                e.wait("scalar", vP, st["sq_red%d" % (tt - 2)])
            e.op("scalar", "activation", out=ln_b[b][:], in_=xw_b[b][:], func=AF.Square,
                 inc=(sP, 1))
            st["p1a_sq%d" % tt] = sP.n
            e.wait("vector", xld, xld.n)
            e.wait("vector", sP, sP.n)
            e.op("vector", "reduce_sum", out=self.sumsq[:, tt:tt + 1], in_=ln_b[b][:], axis=AX.X,
                 inc=(vP, 1))
            st["sq_red%d" % tt] = vP.n
            e.op("vector", "reduce_sum", out=self.sums[:, tt:tt + 1], in_=xw_b[b][:], axis=AX.X,
                 inc=(vP, 1))
            st["st_red%d" % tt] = vP.n
        v_st1 = ln_stat_finish(self.rstd, self.negmr)

        e.wait("scalar", vP, v_st1)
        for tt in range(NT):
            b = tt % 2
            if tt >= 2:
                e.wait("sync", vP, st["lnmax%d" % (tt - 2)])
            else:
                e.wait("sync", vP, st["st_red%d" % (NT - 1)])
                e.wait("sync", sP, st["p1a_sq%d" % (NT - 1)])
            e.dma("sync", xld, out=xw_b[b][:], in_=xr[:, tt, :])
            e.wait("scalar", xld, xld.n)
            if tt >= 2:
                e.wait("scalar", vP, st["lnmax%d" % (tt - 2)])
            e.op("scalar", "activation", out=ln_b[b][:], in_=xw_b[b][:], func=AF.Identity,
                 scale=self.rstd[:, tt:tt + 1], bias=self.negmr[:, tt:tt + 1], inc=(sP, 1))
            e.wait("vector", sP, sP.n)
            e.op("vector", "tensor_reduce", out=self.permax[:, tt:tt + 1], in_=ln_b[b][:], axis=AX.X,
                 op=ALU.max, apply_absolute_value=True, inc=(vP, 1))
            st["lnmax%d" % tt] = vP.n
        e.drain_inc("vector")
        e.op("vector", "tensor_reduce", out=self.lmax1[:], in_=self.permax[:], axis=AX.X, op=ALU.max)
        e.drain_inc("vector", vP, 1)
        g_gmax = self._fold_and_allreduce(ps_small)
        v_sc = self._scale_path(self.clipx_bc, g_gmax, 1.0 / QMAX, ps_small)
        e.op("vector", "tensor_scalar", out=self.stA[:], in0=self.rstd[:], scalar1=self.inv_bc[:, 0:1],
             scalar2=0.0, op0=ALU.mult, op1=ALU.bypass)
        e.op("vector", "tensor_scalar", out=self.stB[:], in0=self.negmr[:], scalar1=self.inv_bc[:, 0:1],
             scalar2=0.0, op0=ALU.mult, op1=ALU.bypass)
        e.drain_inc("vector", vP, 1)
        v_ab = vP.n

        sub1 = ExitStack()
        qxT = s(sub1, ph + "qxT", [128, ND, T], BF16)
        e.wait("scalar", vP, v_ab)
        for tt in range(NT):
            b = tt % 2
            if tt >= 2:
                e.wait("sync", sP, st["qx_act%d" % (tt - 2)])
            else:
                e.wait("sync", vP, st["lnmax%d" % (NT - 1)])
            e.dma("sync", xld, out=xw_b[b][:], in_=xr[:, tt, :])
            e.wait("scalar", xld, xld.n)
            if tt >= 2:
                e.wait("scalar", vP, st["qa_done%d" % (tt - 2)])
                e.wait("scalar", tP, st["qx_tr%d" % (tt - 2)])
            e.op("scalar", "activation", out=ln_b[b][:], in_=xw_b[b][:], func=AF.Identity,
                 scale=self.stA[:, tt:tt + 1], bias=self.stB[:, tt:tt + 1], inc=(sP, 1))
            st["qx_act%d" % tt] = sP.n
            e.wait("vector", sP, sP.n)
            e.op("vector", "tensor_scalar", out=ln_b[b][:], in0=ln_b[b][:], scalar1=MAGIC,
                 scalar2=MAGIC, op0=ALU.add, op1=ALU.subtract)
            e.op("vector", "tensor_scalar", out=qb_b[b][:, 0:D], in0=ln_b[b][:], scalar1=NEG_QMIN,
                 scalar2=QMAX, op0=ALU.max, op1=ALU.min, inc=(vP, 1))
            st["qa_done%d" % tt] = vP.n
            e.wait("tensor", vP, vP.n)
            for dc in range(ND):
                pb = dc % 2
                e.wait_g("tensor", tp_guard[pb])
                e.op("tensor", "transpose", out=ps_tp[pb][:], in_=qb_b[b][:, dc * 128:(dc + 1) * 128],
                     identity=self.identB[:], inc=(tP, 1))
                st["qx_tr%d" % tt] = tP.n
                e.wait("scalar", tP, tP.n)
                e.op("scalar", "copy", out=qxT[:, dc, tt * 128:(tt + 1) * 128], in_=ps_tp[pb][:],
                     inc=(sP, 1))
                tp_guard[pb] = (sP, sP.n)
        s_qx = sP.n
        wg_b = [s(sub1, ph + f"wg{i}", [128, D]) for i in range(2)]
        wu_b = [s(sub1, ph + f"wu{i}", [128, D]) for i in range(2)]
        qwg_b = [s(sub1, ph + f"qwg{i}", [128, ND, 128], BF16) for i in range(2)]
        qwu_b = [s(sub1, ph + f"qwu{i}", [128, ND, 128], BF16) for i in range(2)]
        wt1 = s(sub1, ph + "wt1", [128, max(D, I)])
        coefg = s(sub1, ph + "coefg", [128, 2])
        coefu = s(sub1, ph + "coefu", [128, 2])
        rmax_w = s(sub1, ph + "rmax_w", [128, 2])
        wrec = s(sub1, ph + "wrec", [128, 1])
        silu_b = [s(sub1, ph + f"silu{i}", [128, 512]) for i in range(2)]
        hpre_b = [s(sub1, ph + f"hpre{i}", [128, 1024]) for i in range(2)]
        NHM = NI * (T // 512)
        hmax_arr = s(sub1, ph + "hmax_arr", [128, NHM])
        ps_g = [p(sub1, ph + f"psg{i}", [128, 512]) for i in range(2)]
        ps_u = [p(sub1, ph + f"psu{i}", [128, 512]) for i in range(2)]

        def quant_weight(wraw, wq_out, qwT, coef, par, have_coef, nd):
            e.op("vector", "tensor_reduce", out=rmax_w[:, par:par + 1], in_=wraw[:], axis=AX.X,
                 op=ALU.max, apply_absolute_value=True)
            e.drain_inc("vector")
            e.op("vector", "reciprocal", out=wrec[:], in_=rmax_w[:, par:par + 1])
            e.drain_inc("vector")
            e.op("vector", "tensor_scalar", out=wrec[:], in0=wrec[:], scalar1=QMAX, scalar2=0.0,
                 op0=ALU.mult, op1=ALU.bypass)
            if have_coef:
                e.op("vector", "tensor_tensor", out=coef[:, par:par + 1], in0=rmax_w[:, par:par + 1],
                     in1=self.sc127_bc[:, 0:1], op=ALU.mult)
            e.drain_inc("vector", vP, 1)
            e.wait("scalar", vP, vP.n)
            e.wait("scalar", tP, tP.n)
            e.op("scalar", "activation", out=wq_out[:, 0:nd * 128], in_=wraw[:], func=AF.Identity,
                 scale=wrec[:, 0:1], bias=self.cC[:, 0:1], inc=(sP, 1))
            e.wait("vector", sP, sP.n)
            e.op("vector", "tensor_scalar", out=wq_out[:, 0:nd * 128], in0=wq_out[:, 0:nd * 128],
                 scalar1=MAGIC, scalar2=NEG_QMIN, op0=ALU.subtract, op1=ALU.max)
            e.op("vector", "tensor_scalar", out=qb_b[0][:, 0:nd * 128], in0=wq_out[:, 0:nd * 128],
                 scalar1=QMAX, scalar2=0.0, op0=ALU.min, op1=ALU.bypass, inc=(vP, 1))
            e.wait("tensor", vP, vP.n)
            for dc in range(nd):
                pb = dc % 2
                e.wait_g("tensor", tp_guard[pb])
                e.op("tensor", "transpose", out=ps_tp[pb][:],
                     in_=qb_b[0][:, dc * 128:(dc + 1) * 128],
                     identity=self.identB[:], inc=(tP, 1))
                e.wait("scalar", tP, tP.n)
                e.op("scalar", "copy", out=qwT[:, dc, :],
                     in_=ps_tp[pb][:], inc=(sP, 1))
                tp_guard[pb] = (sP, sP.n)
            return sP.n

        NTC = T // 512
        for it in range(NI):
            b = it % 2
            if it >= 2:
                e.wait("sync", sP, st["wq_done%d" % (it - 2)])
            e.dma("sync", wS, out=wg_b[b][:], in_=self.w["wg"][it * 128:(it + 1) * 128, :])
            e.dma("sync", wS, out=wu_b[b][:], in_=self.w["wu"][it * 128:(it + 1) * 128, :])
            ld_w = wS.n
            e.wait("vector", wS, ld_w)
            e.wait("vector", vP, v_sc)
            if it >= 2:
                e.wait("scalar", tP, st["mm_done%d" % (it - 2)])
                e.wait("vector", sP, st["silu_done%d" % (it - 2)])
            quant_weight(wg_b[b], wt1, qwg_b[b], coefg, b, True, ND)
            s_wq = quant_weight(wu_b[b], wt1, qwu_b[b], coefu, b, True, ND)
            st["wq_done%d" % it] = s_wq
            e.wait("tensor", sP, s_wq)
            if it == 0:
                e.wait("tensor", sP, s_qx)
            for tc in range(NTC):
                pb = tc % 2
                k = it * NTC + tc
                if k >= 2:
                    e.wait("tensor", vP, st["gu_ev%d" % (k - 2)])
                for dc in range(ND):
                    e.op("tensor", "matmul", out=ps_g[pb][:], lhsT=qwg_b[b][:, dc, :],
                         rhs=qxT[:, dc, tc * 512:(tc + 1) * 512],
                         start=(dc == 0), stop=(dc == ND - 1))
                for dc in range(ND):
                    e.op("tensor", "matmul", out=ps_u[pb][:], lhsT=qwu_b[b][:, dc, :],
                         rhs=qxT[:, dc, tc * 512:(tc + 1) * 512],
                         start=(dc == 0), stop=(dc == ND - 1),
                         inc=(tP, 1) if dc == ND - 1 else None)
                st["mm_done%d" % it] = tP.n
                e.wait("scalar", tP, tP.n)
                e.op("scalar", "activation", out=silu_b[pb][:], in_=ps_g[pb][:], func=AF.Silu,
                     scale=coefg[:, b:b + 1], inc=(sP, 1))
                st["silu_done%d" % it] = sP.n
                hb = (tc // 2) % 2
                half = tc % 2
                e.wait("vector", sP, sP.n)
                if "hpre_dma%d" % hb in st:
                    e.wait("vector", hpS, st["hpre_dma%d" % hb])
                e.op("vector", "scalar_tensor_tensor",
                     out=hpre_b[hb][:, half * 512:half * 512 + 512],
                     in0=ps_u[pb][:], scalar=coefu[:, b:b + 1], in1=silu_b[pb][:],
                     op0=ALU.mult, op1=ALU.mult)
                e.op("vector", "tensor_reduce", out=hmax_arr[:, k:k + 1],
                     in_=hpre_b[hb][:, half * 512:half * 512 + 512], axis=AX.X,
                     op=ALU.max, apply_absolute_value=True, inc=(vP, 1))
                st["gu_ev%d" % k] = vP.n
                if half == 1:
                    e.wait("gpsimd", vP, vP.n)
                    e.dma("gpsimd", hpS,
                          out=self.hpre_scr.rearrange("(ic p) t -> p ic t", p=128)[
                              :, it, (tc // 2) * 1024:(tc // 2) * 1024 + 1024],
                          in_=hpre_b[hb][:])
                    st["hpre_dma%d" % hb] = hpS.n
        v_gu = vP.n
        g_hpre = hpS.n

        e.wait("vector", vP, v_gu)
        e.drain_inc("vector")
        e.op("vector", "tensor_reduce", out=self.lmax1[:], in_=hmax_arr[:], axis=AX.X, op=ALU.max)
        e.drain_inc("vector", vP, 1)
        g_gmax2 = self._fold_and_allreduce(ps_small)
        e.wait("tensor", vP, vP.n)
        wd_b = [s(sub1, ph + "wd0", [128, I])]
        NHC = D // 128
        for hc in range(NHC):
            b = 0
            if hc >= 1:
                e.wait("sync", sP, st["wd_done%d" % (hc - 1)])
            e.dma("sync", wdS, out=wd_b[b][:], in_=self.w["wd"][hc * 128:(hc + 1) * 128, :])
            e.wait("vector", wdS, wdS.n)
            if hc >= 1:
                e.wait("vector", tP, st["rv_tr%d" % (hc - 1)])
            e.op("vector", "tensor_reduce", out=rmax_w[:, 0:1], in_=wd_b[b][:], axis=AX.X,
                 op=ALU.max, apply_absolute_value=True)
            e.drain_inc("vector")
            e.op("vector", "reciprocal", out=wrec[:], in_=rmax_w[:, 0:1])
            e.drain_inc("vector")
            e.op("vector", "tensor_scalar", out=wrec[:], in0=wrec[:], scalar1=QMAX, scalar2=0.0,
                 op0=ALU.mult, op1=ALU.bypass)
            e.drain_inc("vector", vP, 1)
            v_rm = vP.n
            e.wait("tensor", vP, v_rm)
            e.op("tensor", "transpose", out=ps_small[0:1, :], in_=rmax_w[:, 0:1],
                 identity=self.identF[:], inc=(tP, 1))
            st["rv_tr%d" % hc] = tP.n
            e.wait("vector", tP, tP.n)
            e.op("vector", "tensor_copy", out=rowvec[0:1, hc * 128:(hc + 1) * 128],
                 in_=ps_small[0:1, :], inc=(vP, 1))
            e.wait("scalar", vP, v_rm)
            e.wait("scalar", tP, tP.n)
            e.op("scalar", "activation", out=wt1[:, 0:I], in_=wd_b[b][:], func=AF.Identity,
                 scale=wrec[:, 0:1], bias=self.cC[:, 0:1], inc=(sP, 1))
            e.wait("vector", sP, sP.n)
            e.op("vector", "tensor_scalar", out=wt1[:, 0:I], in0=wt1[:, 0:I], scalar1=MAGIC,
                 scalar2=NEG_QMIN, op0=ALU.subtract, op1=ALU.max)
            e.op("vector", "tensor_scalar", out=qb_b[1][:, 0:I], in0=wt1[:, 0:I], scalar1=QMAX,
                 scalar2=0.0, op0=ALU.min, op1=ALU.bypass, inc=(vP, 1))
            e.wait("tensor", vP, vP.n)
            for ic in range(NI):
                pb = ic % 2
                e.wait_g("tensor", tp_guard[pb])
                e.op("tensor", "transpose", out=ps_tp[pb][:], in_=qb_b[1][:, ic * 128:(ic + 1) * 128],
                     identity=self.identB[:], inc=(tP, 1))
                e.wait("scalar", tP, tP.n)
                e.op("scalar", "copy", out=qwdT[:, ic, hc * 128:(hc + 1) * 128], in_=ps_tp[pb][:],
                     inc=(sP, 1))
                tp_guard[pb] = (sP, sP.n)
            st["wd_done%d" % hc] = sP.n
        s_wd = sP.n
        v_sch = self._scale_path(self.clipd_bc, g_gmax2, 0.5 / QMAX, ps_small)


        e.wait("vector", sP, s_wd)
        e.wait("vector", vP, v_sch)
        e.drain_inc("vector")
        e.op("vector", "tensor_scalar", out=rowvec[:], in0=rowvec[:],
             scalar1=self.sc127_bc[0:1, 0:1], scalar2=0.0, op0=ALU.mult, op1=ALU.bypass)
        e.drain_inc("vector", vP, 1)
        v_rv = vP.n
        sub1.close()
        sub2 = ExitStack()
        NDC = D // 512
        NIH = (NI + 1) // 2
        ps_d = [p(sub2, ph + f"psd{i}", [128, 512]) for i in range(2 * NDC)]
        hraw_b = [s(sub2, ph + f"hraw{i}", [128, NIH, 512]) for i in range(2)]
        qh_b = [s(sub2, ph + f"qh{i}", [128, NI, 512], BF16) for i in range(2)]
        hq1 = s(sub2, ph + "hq1", [128, 512])
        evt = s(sub2, ph + "evt", [128, 512])
        xo_b = [s(sub2, ph + f"xo{i}", [128, D]) for i in range(2)]

        e.wait("tensor", vP, v_rv)
        for c in range(NDC):
            e.op("tensor", "matmul", out=ps_d[c][:], lhsT=self.ones_r[:],
                 rhs=rowvec[0:1, c * 512:(c + 1) * 512], start=True, stop=True,
                 inc=(tP, 1) if c == NDC - 1 else None)
        e.wait("vector", tP, tP.n)
        for c in range(NDC):
            e.op("vector", "tensor_copy", out=ROWSCALE[:, c * 512:(c + 1) * 512],
                 in_=ps_d[c][:], inc=(vP, 1) if c == NDC - 1 else None)
        v_rs = vP.n

        hsr = self.hpre_scr.rearrange("(ic p) t -> p ic t", p=128)
        NTC2 = T // 512
        v_invh = v_sch
        for tc in range(NTC2):
            b = tc % 2
            for j in range(2):
                lo = j * NIH
                hi = min(NI, lo + NIH)
                if tc >= 2:
                    e.wait("sync", sP, st["hq_h%d_%d" % (tc - 2, j)])
                if tc == 0 and j == 0:
                    e.wait("sync", hpS, g_hpre)
                e.dma("sync", hrS, out=hraw_b[j][:, 0:hi - lo, :],
                      in_=hsr[:, lo:hi, tc * 512:(tc + 1) * 512])
                st["hld%d_%d" % (tc, j)] = hrS.n
            e.wait("scalar", vP, v_invh)
            e.wait("scalar", vP, vP.n)
            if tc >= 2:
                e.wait("scalar", tP, st["dmm%d" % (tc - 2)])
            for ic in range(NI):
                j = ic // NIH
                if ic % NIH == 0:
                    e.wait("scalar", hrS, st["hld%d_%d" % (tc, j)])
                e.op("scalar", "activation", out=hq1[:], in_=hraw_b[j][:, ic - j * NIH, :],
                     func=AF.Identity, scale=self.inv_bc[:, 0:1], bias=self.cC[:, 0:1],
                     inc=(sP, 1))
                if ic == min(NI, NIH) - 1:
                    st["hq_h%d_0" % tc] = sP.n
                e.wait("vector", sP, sP.n)
                e.op("vector", "tensor_scalar", out=hq1[:], in0=hq1[:], scalar1=MAGIC,
                     scalar2=NEG_QMIN, op0=ALU.subtract, op1=ALU.max)
                e.op("vector", "tensor_scalar", out=qh_b[b][:, ic, :], in0=hq1[:], scalar1=QMAX,
                     scalar2=0.0, op0=ALU.min, op1=ALU.bypass, inc=(vP, 1))
                e.wait("scalar", vP, vP.n)
            st["hq_h%d_1" % tc] = sP.n
            v_qh = vP.n
            e.wait("tensor", vP, v_qh)
            if tc == 0:
                e.wait("tensor", sP, s_wd)
            for tl in range(4):
                tt = tc * 4 + tl
                bo = tl % 2
                pd = [ps_d[(tl % 2) * NDC + c] for c in range(NDC)]
                if tt >= 2:
                    e.wait("tensor", vP, st["d_ev%d" % (tt - 2)])
                for ic in range(NI):
                    for c in range(NDC):
                        e.op("tensor", "matmul", out=pd[c][:],
                             lhsT=qh_b[b][:, ic, tl * 128:(tl + 1) * 128],
                             rhs=qwdT[:, ic, c * 512:(c + 1) * 512],
                             start=(ic == 0), stop=(ic == NI - 1),
                             inc=(tP, 1) if (ic == NI - 1 and c == NDC - 1) else None)
                st["dmm%d" % tc] = tP.n
                if tt >= 2:
                    e.wait("sync", vP, st["d_ev%d" % (tt - 2)])
                else:
                    e.wait("sync", sP, st["qx_act%d" % (NT - 1)])
                e.dma("sync", xld, out=xw_b[bo][:], in_=xr[:, tt, :])
                ld_x = xld.n
                e.wait("vector", tP, tP.n)
                e.wait("vector", vP, v_rs)
                e.wait("vector", xld, ld_x)
                e.wait("vector", sP, sP.n)
                if tt >= 2:
                    e.wait("vector", xoS, st["xo_dma%d" % (tt - 2)])
                for c in range(NDC):
                    e.op("vector", "tensor_tensor", out=evt[:], in0=pd[c][:],
                         in1=ROWSCALE[:, c * 512:(c + 1) * 512], op=ALU.mult)
                    e.op("vector", "tensor_tensor", out=xo_b[bo][:, c * 512:(c + 1) * 512],
                         in0=xw_b[bo][:, c * 512:(c + 1) * 512], in1=evt[:],
                         op=ALU.add, inc=(vP, 1) if c == NDC - 1 else None)
                st["d_ev%d" % tt] = vP.n
                if ln2_stats:
                    e.wait("scalar", vP, vP.n)
                    if tt >= 2:
                        e.wait("scalar", vP, st["sq2_red%d" % (tt - 2)])
                    e.op("scalar", "activation", out=hq1[:], in_=xo_b[bo][:, 0:512],
                         func=AF.Square, inc=(sP, 1))
                    e.op("scalar", "activation", out=evt[:], in_=xo_b[bo][:, 512:D],
                         func=AF.Square, inc=(sP, 1)) if D > 512 else None
                    e.wait("vector", sP, sP.n)
                    e.op("vector", "reduce_sum", out=self.sumsq[:, tt:tt + 1], in_=hq1[:],
                         axis=AX.X)
                    if D > 512:
                        e.op("vector", "reduce_sum", out=self.msq16[:, 0:1], in_=evt[:], axis=AX.X)
                        e.drain_inc("vector")
                        e.op("vector", "tensor_tensor", out=self.sumsq[:, tt:tt + 1],
                             in0=self.sumsq[:, tt:tt + 1], in1=self.msq16[:, 0:1], op=ALU.add)
                    e.drain_inc("vector", vP, 1)
                    st["sq2_red%d" % tt] = vP.n
                    e.op("vector", "reduce_sum", out=self.sums[:, tt:tt + 1], in_=xo_b[bo][:],
                         axis=AX.X, inc=(vP, 1))
                e.wait("gpsimd", vP, vP.n)
                e.dma("gpsimd", xoS,
                      out=write_out.rearrange("(tt p) d -> p tt d", p=128)[:, tt, :],
                      in_=xo_b[bo][:])
                st["xo_dma%d" % tt] = xoS.n
        if ln2_stats:
            ln_stat_finish(self.rstd2, self.negmr2)
        e.wait("gpsimd", xoS, xoS.n)
        self.env.barrier(self.bar)
        sub2.close()
        pes.close()

    def emit_attn(self):
        cfg, nc, e = self.cfg, self.nc, self.env
        T, D, H, DH, NT, ND = cfg.T, cfg.D, cfg.H, cfg.DH, cfg.NT, cfg.ND
        NH2 = DH // 2
        NDC = D // 512
        HPC = 512 // DH
        st = {}
        tP, vP, sP = self.tP, self.vP, self.sP
        e2 = self.env
        axS = e2.sem("axS")
        awS = e2.sem("awS")
        qrS = e2.sem("qrS")
        qcS = e2.sem("qcS")
        x2S = e2.sem("x2S")
        pes = ExitStack()
        s, p = self.sb, self.ps

        KrT = s(pes, "KrT", [128, ND, T])
        VS = 66
        Varena = s(pes, "Varena", [128, NT, H * VS])
        WT = s(pes, "WT", [128, ND, D])
        sub_pj = ExitStack()
        cosT = s(sub_pj, "cosT", [128, NT, DH // 2])
        sinT = s(sub_pj, "sinT", [128, NT, DH // 2])
        csS = e2.sem("csS")
        e.dma("sync", csS, out=cosT[:], in_=self.cos_d.rearrange("p t i -> p (t i)"))
        e.dma("sync", csS, out=sinT[:], in_=self.sin_d.rearrange("p t i -> p (t i)"))
        cs_ld = csS.n
        lnT = s(sub_pj, "lnT", [128, ND, 128])
        x1_b = [s(sub_pj, f"ax1_{i}", [128, D]) for i in range(2)]
        ln2_b = [s(sub_pj, f"aln_{i}", [128, D]) for i in range(2)]
        pr_b = ln2_b
        rp_t = s(sub_pj, "rp_t", [128, 512])
        w_b = [s(sub_pj, f"aw_{i}", [128, D]) for i in range(2)]
        ps_pj = [p(sub_pj, "ps_pj%d" % i, [128, 512]) for i in range(2)]
        ps_tp = [p(sub_pj, "ps_tp%d" % i, [128, 128]) for i in range(2)]
        tp_guard = [None, None]
        x2pS = e2.sem("x2pS")
        e.dma("gpsimd", x2pS, out=self.x2_scr, in_=self.x1_scr)

        e.op("vector", "memset",
             ap=Varena[:].rearrange("p t (h s) -> p t h s", h=H)[:, :, :, 64:65],
             constant=1.0, inc=(vP, 1))
        def prep_WT(w_ap):
            e.wait("scalar", tP, tP.n)
            for oc in range(ND):
                b = oc % 2
                if oc >= 2:
                    e.wait("sync", tP, st["aWtr%d" % (oc - 2)])
                e.dma("sync", awS, out=w_b[b][:], in_=w_ap[oc * 128:(oc + 1) * 128, :])
                e.wait("tensor", awS, awS.n)
                for dcc in range(ND):
                    pb = dcc % 2
                    e.wait_g("tensor", tp_guard[pb])
                    e.op("tensor", "transpose", out=ps_tp[pb][:],
                         in_=w_b[b][:, dcc * 128:(dcc + 1) * 128], identity=self.identF[:],
                         inc=(tP, 1))
                    st["aWtr%d" % oc] = tP.n
                    e.wait("scalar", tP, tP.n)
                    e.op("scalar", "copy", out=WT[:, dcc, oc * 128:(oc + 1) * 128], in_=ps_tp[pb][:],
                         inc=(sP, 1))
                    tp_guard[pb] = (sP, sP.n)
            return sP.n

        def pass_proj(w_ap, kind):
            s_w = prep_WT(w_ap)
            for tt in range(NT):
                b = tt % 2
                if tt >= 2:
                    e.wait("sync", sP, st["aln_done%d" % (tt - 2)])
                e.dma("sync", axS, out=x1_b[b][:],
                      in_=self.x1_scr.rearrange("(tt p) d -> p tt d", p=128)[:, tt, :])
                e.wait("scalar", axS, axS.n)
                e.wait("scalar", tP, tP.n)
                e.op("scalar", "activation", out=ln2_b[b][:], in_=x1_b[b][:], func=AF.Identity,
                     scale=self.rstd2[:, tt:tt + 1], bias=self.negmr2[:, tt:tt + 1], inc=(sP, 1))
                st["aln_done%d" % tt] = sP.n
                e.wait("tensor", sP, sP.n)
                for dc in range(ND):
                    pb = dc % 2
                    e.wait_g("tensor", tp_guard[pb])
                    e.op("tensor", "transpose", out=ps_tp[pb][:],
                         in_=ln2_b[b][:, dc * 128:(dc + 1) * 128], identity=self.identF[:],
                         inc=(tP, 1))
                    e.wait("vector", tP, tP.n)
                    if kind == "Q" and "aQdma%d" % (tt - 1) in st:
                        e.wait("vector", qrS, st["aQdma%d" % (tt - 1)])
                    e.op("vector", "tensor_copy", out=lnT[:, dc, :], in_=ps_tp[pb][:],
                         inc=(vP, 1))
                    tp_guard[pb] = (vP, vP.n)
                v_lnT = vP.n
                e.wait("tensor", vP, v_lnT)
                if tt == 0:
                    e.wait("tensor", sP, s_w)
                for c in range(NDC):
                    for dc in range(ND):
                        e.op("tensor", "matmul", out=ps_pj[c][:], lhsT=lnT[:, dc, :],
                             rhs=WT[:, dc, c * 512:(c + 1) * 512],
                             start=(dc == 0), stop=(dc == ND - 1),
                             inc=(tP, 1) if dc == ND - 1 else None)
                mm_n = tP.n
                if kind == "V":
                    e.wait("vector", tP, mm_n)
                    for h in range(H):
                        c = (h * 64) // 512
                        off = (h * 64) % 512
                        e.op("vector", "tensor_copy",
                             out=Varena[:, tt, h * VS:h * VS + 64],
                             in_=ps_pj[c][:, off:off + 64],
                             inc=(vP, 1) if h == H - 1 else None)
                else:
                    e.wait("vector", tP, mm_n)
                    if tt >= 2:
                        e.wait("vector", tP, st["aRtr%d" % (tt - 2)])
                    e.wait("vector", csS, cs_ld)
                    cosb = cosT[:, tt, :].unsqueeze(1).broadcast_to([128, HPC, NH2])
                    sinb = sinT[:, tt, :].unsqueeze(1).broadcast_to([128, HPC, NH2])
                    for c in range(NDC):
                        pj = ps_pj[c][:].rearrange("p (h i two) -> p h i two", h=HPC, two=2)
                        pr = pr_b[b][:, c * 512:(c + 1) * 512].rearrange(
                            "p (h i two) -> p h i two", h=HPC, two=2)
                        rp = rp_t[:, 0:HPC * NH2].rearrange("p (h i) -> p h i", h=HPC)
                        e.op("vector", "tensor_tensor", out=pr[:, :, :, 0], in0=pj[:, :, :, 0],
                             in1=cosb, op=ALU.mult)
                        e.op("vector", "tensor_tensor", out=rp, in0=pj[:, :, :, 1], in1=sinb,
                             op=ALU.mult)
                        e.op("vector", "tensor_tensor", out=pr[:, :, :, 0], in0=pr[:, :, :, 0],
                             in1=rp, op=ALU.subtract)
                        e.op("vector", "tensor_tensor", out=pr[:, :, :, 1], in0=pj[:, :, :, 0],
                             in1=sinb, op=ALU.mult)
                        e.op("vector", "tensor_tensor", out=rp, in0=pj[:, :, :, 1], in1=cosb,
                             op=ALU.mult)
                        e.op("vector", "tensor_tensor", out=pr[:, :, :, 1], in0=pr[:, :, :, 1],
                             in1=rp, op=ALU.add, inc=(vP, 1))
                    v_rope = vP.n
                    e.wait("tensor", vP, v_rope)
                    for dc in range(ND):
                        pb = dc % 2
                        e.wait_g("tensor", tp_guard[pb])
                        e.op("tensor", "transpose", out=ps_tp[pb][:],
                             in_=pr_b[b][:, dc * 128:(dc + 1) * 128], identity=self.identF[:],
                             inc=(tP, 1))
                        st["aRtr%d" % tt] = tP.n
                        e.wait("scalar", tP, tP.n)
                        if kind == "K":
                            e.op("scalar", "copy", out=KrT[:, dc, tt * 128:(tt + 1) * 128],
                                 in_=ps_tp[pb][:], inc=(sP, 1))
                        else:
                            if dc == 0 and "aQdma%d" % (tt - 1) in st:
                                e.wait("scalar", qrS, st["aQdma%d" % (tt - 1)])
                            e.op("scalar", "copy", out=lnT[:, dc, :], in_=ps_tp[pb][:], inc=(sP, 1))
                        tp_guard[pb] = (sP, sP.n)
                    if kind == "Q":
                        e.wait("gpsimd", sP, sP.n)
                        e.dma("gpsimd", qrS,
                              out=self.qr_scr.rearrange("(dc p) t -> p dc t", p=128)[
                                  :, :, tt * 128:(tt + 1) * 128],
                              in_=lnT[:])
                        st["aQdma%d" % tt] = qrS.n

        pass_proj(self.w["wk"], "K")
        s_kr = sP.n
        pass_proj(self.w["wv"], "V")
        v_kv = vP.n
        pass_proj(self.w["wq"], "Q")
        g_qr = qrS.n
        s_wo = prep_WT(self.w["wo"])
        sub_pj.close()

        QrC = s(pes, "QrC", [128, ND, 512])
        E_b = [s(pes, f"aE{i}", [128, 512]) for i in range(2)]
        Dbc = s(pes, "Dbc", [64, 512])
        dinv = Dbc[0:1, :]
        headsN = s(pes, "headsN", [128, ND, 512])
        x2_b = [s(pes, "ax2s", [128, D])] * 2
        ps_s = [p(pes, "ps_s%d" % i, [128, 512]) for i in range(2)]
        ps_av = [p(pes, "ps_av%d" % i, [128, 512]) for i in range(2)]
        ps_o = [p(pes, "ps_o%d" % i, [128, 512]) for i in range(2)]

        g_ps_s = [None, None]
        g_ps_av = [None, None]
        g_ps_o = [None, None]
        g_E = [None, None]
        g_x2 = [None, None]

        NQC = T // 512
        e.wait("tensor", sP, s_wo)
        e.wait("tensor", vP, v_kv)
        for qc in range(NQC):
            if qc >= 1:
                e.wait("sync", tP, st["aQcons%d" % (qc - 1)])
            e.wait("sync", qrS, g_qr)
            e.dma("sync", qcS, out=QrC[:],
                  in_=self.qr_scr.rearrange("(dc p) t -> p dc t", p=128)[
                      :, :, qc * 512:(qc + 1) * 512])
            ld_q = qcS.n
            e.wait("tensor", qcS, ld_q)
            nkt = qc * 4 + 4
            for h in range(H):
                dc = h // 2
                pb2 = (h % 2) * 64
                avp = ps_av[h % 2]
                e.wait_g("tensor", g_ps_av[h % 2])
                for kt in range(nkt):
                    j = kt - qc * 4
                    q0 = max(0, j * 128)
                    qn = 512 - q0
                    slot = kt % 2
                    e.wait_g("tensor", g_ps_s[slot])
                    e.op("tensor", "matmul", out=ps_s[slot][:, 0:qn],
                         lhsT=KrT[pb2:pb2 + 64, dc, kt * 128:(kt + 1) * 128],
                         rhs=QrC[pb2:pb2 + 64, dc, q0:512],
                         start=True, stop=True, inc=(tP, 1))
                    e.wait("scalar", tP, tP.n)
                    e.wait_g("scalar", g_E[slot])
                    e.op("scalar", "activation", out=E_b[slot][:, 0:qn], in_=ps_s[slot][:, 0:qn],
                         func=AF.Exp, scale=0.125, inc=(sP, 1))
                    g_ps_s[slot] = (sP, sP.n)
                    if j >= 0:
                        e.wait("vector", sP, sP.n)
                        e.op("vector", "tensor_tensor", out=E_b[slot][:, 0:128],
                             in0=E_b[slot][:, 0:128], in1=self.tri[:], op=ALU.mult,
                             inc=(vP, 1))
                        e_ready = (vP, vP.n)
                    else:
                        e_ready = (sP, sP.n)
                    e.wait_g("tensor", e_ready)
                    e.op("tensor", "matmul", out=avp[0:65, q0:512],
                         lhsT=Varena[:, kt, h * VS:h * VS + 65],
                         rhs=E_b[slot][:, 0:qn],
                         start=(kt == 0), stop=(kt == nkt - 1), inc=(tP, 1))
                    g_E[slot] = (tP, tP.n)
                e.wait("vector", tP, tP.n)
                e.op("vector", "tensor_copy", out=dinv, in_=avp[64:65, :])
                e.drain_inc("vector")
                e.op("vector", "reciprocal", out=dinv, in_=dinv)
                e.drain_inc("vector", vP, 1)
                e.wait_g("tensor", g_ps_s[0])
                e.wait("tensor", vP, vP.n)
                e.op("tensor", "matmul", out=ps_s[0][0:64, :], lhsT=self.ones_r[0:1, 0:64],
                     rhs=dinv, start=True, stop=True, inc=(tP, 1))
                e.wait("vector", tP, tP.n)
                if qc > 0 and h < 2:
                    e.wait("vector", tP, st["aOmm%d" % (qc - 1)])
                e.op("vector", "tensor_copy", out=Dbc[:], in_=ps_s[0][0:64, :])
                e.op("vector", "tensor_tensor", out=headsN[pb2:pb2 + 64, dc, :],
                     in0=avp[0:64, :], in1=Dbc[:], op=ALU.mult, inc=(vP, 1))
                g_ps_s[0] = (vP, vP.n)
                g_ps_av[h % 2] = (vP, vP.n)
            v_hN = vP.n
            st["aQcons%d" % qc] = tP.n
            e.wait("tensor", vP, v_hN)
            if qc == 0:
                e.wait("tensor", sP, s_wo)
            for tl in range(4):
                tt = qc * 4 + tl
                b = tl % 2
                for c in range(NDC):
                    e.wait_g("tensor", g_ps_o[c])
                    for dcc in range(ND):
                        e.op("tensor", "matmul", out=ps_o[c][:],
                             lhsT=headsN[:, dcc, tl * 128:(tl + 1) * 128],
                             rhs=WT[:, dcc, c * 512:(c + 1) * 512],
                             start=(dcc == 0), stop=(dcc == ND - 1),
                             inc=(tP, 1) if dcc == ND - 1 else None)
                st["aOmm%d" % qc] = tP.n
                mm_o = tP.n
                e.wait("vector", tP, mm_o)
                e.wait_g("vector", g_x2[b])
                for c in range(NDC):
                    e.op("vector", "tensor_copy", out=x2_b[b][:, c * 512:(c + 1) * 512],
                         in_=ps_o[c][:], inc=(vP, 1) if c == NDC - 1 else None)
                st["ax2ev%d" % tt] = vP.n
                g_ps_o[0] = (vP, vP.n)
                g_ps_o[1] = (vP, vP.n)
                e.wait("gpsimd", vP, vP.n)
                if qc == 0 and tl == 0:
                    e.wait("gpsimd", x2pS, x2pS.n)
                e.dma("gpsimd", x2S, accum=True,
                      out=self.x2_scr.rearrange("(tt p) d -> p tt d", p=128)[:, tt, :],
                      in_=x2_b[b][:])
                g_x2[0] = g_x2[1] = (x2S, x2S.n)
        e.wait("gpsimd", x2S, x2S.n)
        self.env.barrier(self.bar)
        pes.close()


def make_inputs(cfg, full_inputs):
    X = np.ascontiguousarray(np.asarray(full_inputs["L0_attn_X"], dtype=np.float32))
    base = {
        "wq": np.ascontiguousarray(np.asarray(full_inputs["L0_attn_W_q"], np.float32)),
        "wk": np.ascontiguousarray(np.asarray(full_inputs["L0_attn_W_k"], np.float32)),
        "wv": np.ascontiguousarray(np.asarray(full_inputs["L0_attn_W_v"], np.float32)),
        "wo": np.ascontiguousarray(np.asarray(full_inputs["L0_attn_W_o"], np.float32)),
        "wg": np.ascontiguousarray(np.asarray(full_inputs["L0_ffn_w_gate"], np.float32)),
        "wu": np.ascontiguousarray(np.asarray(full_inputs["L0_ffn_w_up"], np.float32)),
        "wd": np.ascontiguousarray(np.asarray(full_inputs["L0_ffn_w_down"], np.float32)),
        "clipx": np.asarray(full_inputs["L0_ffn_clip_x"], np.float32).reshape(1, 1),
        "clipd": np.asarray(full_inputs["L0_ffn_clip_down"], np.float32).reshape(1, 1),
    }
    maps = []
    for c in range(cfg.n_cores):
        m = dict(base)
        m["x"] = np.ascontiguousarray(X[c])
        maps.append(m)
    return maps


_CFG = Cfg(T=2048, D=1024, H=16, I=2816, n_cores=8)
_MK_CACHE = {}


def _get_mk():
    if "mk" not in _MK_CACHE:
        _MK_CACHE["mk"] = MK(_CFG, with_attn=True)
    return _MK_CACHE["mk"]


def kernel(L0_attn_X, L0_attn_W_q, L0_attn_W_k, L0_attn_W_v, L0_attn_W_o,
           L0_ffn_clip_x, L0_ffn_clip_down, L0_ffn_w_gate, L0_ffn_w_up, L0_ffn_w_down):
    full = {
        "L0_attn_X": L0_attn_X, "L0_attn_W_q": L0_attn_W_q, "L0_attn_W_k": L0_attn_W_k,
        "L0_attn_W_v": L0_attn_W_v, "L0_attn_W_o": L0_attn_W_o,
        "L0_ffn_clip_x": L0_ffn_clip_x, "L0_ffn_clip_down": L0_ffn_clip_down,
        "L0_ffn_w_gate": L0_ffn_w_gate, "L0_ffn_w_up": L0_ffn_w_up,
        "L0_ffn_w_down": L0_ffn_w_down,
    }
    m = _get_mk()
    in_maps = make_inputs(_CFG, full)
    res = run_bass_kernel_spmd(m.nc, in_maps, core_ids=list(range(_CFG.n_cores)))
    return np.stack([np.asarray(res.results[i]["out"]) for i in range(_CFG.n_cores)], axis=0)


# version 03
# speedup vs baseline: 1.1074x; 1.1074x over previous
import numpy as np
import ml_dtypes
import concourse.bass as bass
import concourse.mybir as mybir
from concourse.bass_utils import run_bass_kernel_spmd
from contextlib import ExitStack

F32 = mybir.dt.float32
BF16 = mybir.dt.bfloat16
AF = mybir.ActivationFunctionType
ALU = mybir.AluOpType
AX = mybir.AxisListType

MAGIC = float(np.float32(1.5 * 2 ** 23))
NEG_QMIN = -128.0
QMAX = 127.0


class Cfg:
    def __init__(self, T=2048, D=1024, H=16, I=2816, n_cores=8):
        self.T, self.D, self.H, self.I = T, D, H, I
        self.n_cores = n_cores
        self.DH = D // H
        self.NT = T // 128
        self.ND = D // 128
        self.NI = I // 128
        self.eps = 1e-5


class Sem:
    def __init__(self, nc, es, name):
        self.h = es.enter_context(nc.semaphore(name))
        self.n = 0

    def bump(self, k=1):
        self.n += k
        return self.n


class Env:
    ENGINES = ("sync", "tensor", "vector", "scalar", "gpsimd")

    def __init__(self, nc, es):
        self.nc = nc
        self.es = es
        self.ops = {e: [] for e in self.ENGINES}

    def sem(self, name):
        return Sem(self.nc, self.es, name)

    def emit(self, eng, f):
        self.ops[eng].append(f)

    def op(self, eng, meth, *, inc=None, **kw):
        if inc is not None:
            sem, k = inc
            sem.bump(k)
            tgt = (sem.h, k)
        else:
            tgt = None

        def f(e, meth=meth, kw=kw, tgt=tgt):
            ins = getattr(e, meth)(**kw)
            if tgt is not None:
                ins.then_inc(tgt[0], tgt[1])
        self.emit(eng, f)

    def wait(self, eng, sem, val=None):
        v = sem.n if val is None else val
        if v <= 0:
            return
        self.emit(eng, lambda e, h=sem.h, v=v: e.wait_ge(h, v))

    def wait_g(self, eng, guard):
        if guard is None:
            return
        sem, v = guard
        self.wait(eng, sem, v)

    def dma_dbg(self, eng, sem, *, out, in_):
        if sem.n > 0:
            self.wait(eng, sem, sem.n)
        sem.bump(16)
        tgt = sem.h

        def f(e, out=out, in_=in_, tgt=tgt):
            e.dma_start(out=out, in_=in_, allow_slow_non_contiguous=True).then_inc(tgt, 16)
        self.emit(eng, f)

    def dma(self, eng, sem, *, out, in_, throttle=True, accum=False):
        if throttle and sem.n > 0:
            self.wait(eng, sem, sem.n)
        sem.bump(16)
        tgt = sem.h

        def f(e, out=out, in_=in_, tgt=tgt, accum=accum):
            if accum:
                e.dma_start(out=out, in_=in_, accum_op=ALU.add).then_inc(tgt, 16)
            else:
                e.dma_start(out=out, in_=in_).then_inc(tgt, 16)
        self.emit(eng, f)

    def drain_inc(self, eng, sem=None, k=1):
        if sem is not None:
            sem.bump(k)
            self.emit(eng, lambda e, h=sem.h, k=k: e.drain().then_inc(h, k))
        else:
            self.emit(eng, lambda e: e.drain())

    def barrier(self, bar):
        base = bar.n
        for e in self.ENGINES:
            bar.bump(1)
            self.emit(e, lambda en, h=bar.h: en.sem_inc(h, 1))
        for e in self.ENGINES:
            self.wait(e, bar, base + len(self.ENGINES))

    def replay(self):
        with self.nc.Block() as block:
            for eng in self.ENGINES:
                if not self.ops[eng]:
                    continue
                deco = getattr(block, eng)

                def body(e, eng=eng):
                    for f in self.ops[eng]:
                        f(e)
                deco(body)


def build_consts(cfg):
    c = {}
    c["ident_f32"] = np.eye(128, dtype=np.float32)
    c["ident_bf16"] = np.eye(128, dtype=np.float32).astype(ml_dtypes.bfloat16)
    k = np.arange(128)[:, None]
    q = np.arange(128)[None, :]
    c["tri"] = (q >= k).astype(np.float32)
    DH = cfg.DH
    theta = (10000.0 ** (-2.0 * np.arange(DH // 2, dtype=np.float32) / DH)).astype(np.float32)
    pos = np.arange(cfg.T, dtype=np.float32)
    ang = (pos[:, None] * theta[None, :]).astype(np.float32)
    c["cos"] = np.cos(ang).astype(np.float32).reshape(cfg.NT, 128, DH // 2).transpose(1, 0, 2).copy()
    c["sin"] = np.sin(ang).astype(np.float32).reshape(cfg.NT, 128, DH // 2).transpose(1, 0, 2).copy()
    return c


class MK:
    def __init__(self, cfg, with_attn=True):
        self.cfg = cfg
        self.debug_taps = False
        self.with_attn = with_attn
        self.nc = bass.Bass("TRN2", target_bir_lowering=False, detect_race_conditions=False)
        self.es = ExitStack()
        self.env = Env(self.nc, self.es)
        self.ccin_dG = 0
        self._build()

    def sb(self, es, name, shape, dt=F32):
        return es.enter_context(self.nc.sbuf_tensor(name, shape, dt))

    def ps(self, es, name, shape):
        return es.enter_context(self.nc.psum_tensor(name, shape, F32))

    def _build(self):
        cfg, nc, env, es = self.cfg, self.nc, self.env, self.es
        T, D, H, I = cfg.T, cfg.D, cfg.H, cfg.I
        NT, ND, NI, DH = cfg.NT, cfg.ND, cfg.NI, cfg.DH

        self.x_in = nc.dram_tensor("x", [T, D], F32, kind="ExternalInput").ap()
        self.w = {}
        for nm, shp in [("wq", [D, D]), ("wk", [D, D]), ("wv", [D, D]), ("wo", [D, D]),
                        ("wg", [I, D]), ("wu", [I, D]), ("wd", [D, I])]:
            self.w[nm] = nc.dram_tensor(nm, shp, F32, kind="ExternalInput").ap()
        self.clipx = nc.dram_tensor("clipx", [1, 1], F32, kind="ExternalInput").ap()
        self.clipd = nc.dram_tensor("clipd", [1, 1], F32, kind="ExternalInput").ap()
        self.out = nc.dram_tensor("out", [T, D], F32, kind="ExternalOutput").ap()

        self.hpre_scr = nc.dram_tensor("hpre_scr", [I, T], F32, kind="Internal").ap()
        self.x1_scr = nc.dram_tensor("x1_scr", [T, D], F32, kind="Internal").ap()
        self.x2_scr = nc.dram_tensor("x2_scr", [T, D], F32, kind="Internal").ap()
        self.qr_scr = nc.dram_tensor("qr_scr", [D, T], F32, kind="Internal").ap()
        self.cc_in = nc.dram_tensor("cc_in", [1, 16], F32, kind="Internal")
        self.cc_out = nc.dram_tensor("cc_out", [1, 16], F32, kind="Internal")

        cst = build_consts(cfg)
        ident_f_d = nc.inline_tensor(cst["ident_f32"], name="ident_f").ap()
        ident_b_d = nc.inline_tensor(cst["ident_bf16"], name="ident_b").ap()
        tri_d = nc.inline_tensor(cst["tri"], name="tri_c").ap()
        cos_d = nc.inline_tensor(cst["cos"], name="cos_c").ap()
        sin_d = nc.inline_tensor(cst["sin"], name="sin_c").ap()

        s = self.sb
        self.identF = s(es, "identF", [128, 128])
        self.identB = s(es, "identB", [128, 128], BF16)
        self.tri = s(es, "tri", [128, 128])
        self.cC = s(es, "cC", [128, 1])
        self.ones_r = s(es, "ones_r", [1, 128])
        self.cc_sb = s(es, "cc_sb", [1, 16])
        self.gmax_bc = s(es, "gmax_bc", [128, 1])
        self.clipx_bc = s(es, "clipx_bc", [128, 1])
        self.clipd_bc = s(es, "clipd_bc", [128, 1])
        self.sig_t = s(es, "sig_t", [128, 1])
        self.scale_t = s(es, "scale_t", [128, 1])
        self.inv_bc = s(es, "inv_bc", [128, 1])
        self.sc127_bc = s(es, "sc127_bc", [128, 1])
        self.sums = s(es, "sums", [128, NT])
        self.sumsq = s(es, "sumsq", [128, NT])
        self.mean16 = s(es, "mean16", [128, NT])
        self.msq16 = s(es, "msq16", [128, NT])
        self.rstd = s(es, "rstd", [128, NT])
        self.negmr = s(es, "negmr", [128, NT])
        self.stA = s(es, "stA", [128, NT])
        self.stB = s(es, "stB", [128, NT])
        self.permax = s(es, "permax", [128, NT])
        self.lmax1 = s(es, "lmax1", [128, 1])
        self.rstd2 = s(es, "rstd2", [128, NT])
        self.negmr2 = s(es, "negmr2", [128, NT])

        e = env
        self.initS = e.sem("initS")
        self.ccD = e.sem("ccD")
        self.tP = e.sem("tP")
        self.vP = e.sem("vP")
        self.sP = e.sem("sP")
        self.ccS = e.sem("ccS")
        self.bar = e.sem("bar")

        self.cos_d, self.sin_d = cos_d, sin_d
        for dst, srct in [(self.identF, ident_f_d), (self.identB, ident_b_d), (self.tri, tri_d)]:
            e.dma("sync", self.initS, out=dst[:], in_=srct)
        e.dma("sync", self.initS, out=self.clipx_bc[:], in_=self.clipx.broadcast_to([128, 1]))
        e.dma("sync", self.initS, out=self.clipd_bc[:], in_=self.clipd.broadcast_to([128, 1]))
        self.eps_bc = s(es, "eps_bc", [128, 1])
        e.op("vector", "memset", ap=self.eps_bc[:], constant=self.cfg.eps)
        nc.const_aps.register(F32, self.cfg.eps, self.eps_bc[:])
        e.op("vector", "memset", ap=self.cC[:], constant=MAGIC)
        e.op("vector", "memset", ap=self.ones_r[:], constant=1.0)
        e.op("vector", "memset", ap=self.cc_sb[:], constant=0.0)
        e.drain_inc("vector", self.vP, 1)
        self.init_v = self.vP.n

        self.emit_ffn("f0", self.x_in,
                      write_out=self.x1_scr if self.with_attn else self.out,
                      ln2_stats=self.with_attn)
        if self.with_attn:
            self.emit_attn()
            self.emit_ffn("f1", self.x2_scr, write_out=self.out, ln2_stats=False)
        env.barrier(self.bar)
        env.replay()
        es.close()

    def _fold_and_allreduce(self, ps_small):
        e, tP, vP, ccD, ccS = self.env, self.tP, self.vP, self.ccD, self.ccS
        e.wait("tensor", vP, vP.n)
        e.wait("tensor", self.initS, self.initS.n)
        e.op("tensor", "transpose", out=ps_small[0:1, :], in_=self.lmax1[:],
             identity=self.identF[:], inc=(tP, 1))
        e.wait("vector", tP, tP.n)
        e.wait("vector", ccD, self.ccin_dG)
        e.wait("vector", self.vP, self.init_v)
        e.op("vector", "tensor_reduce", out=self.cc_sb[0:1, 0:1], in_=ps_small[0:1, :], axis=AX.X,
             op=ALU.max)
        e.drain_inc("vector", vP, 1)
        e.wait("gpsimd", vP, vP.n)
        e.dma("gpsimd", ccD, out=self.cc_in.ap(), in_=self.cc_sb[:])
        self.ccin_dG = ccD.n
        e.wait("gpsimd", ccD, ccD.n)
        e.op("gpsimd", "collective_compute",
             kind="AllReduce", op=ALU.max, replica_groups=[list(range(self.cfg.n_cores))],
             ins=[self.cc_in.ap().opt()], outs=[self.cc_out.ap().opt()], inc=(ccS, 1))
        e.wait("gpsimd", ccS, ccS.n)
        e.dma("gpsimd", ccD, out=self.gmax_bc[:],
              in_=self.cc_out.ap()[0:1, 0:1].broadcast_to([128, 1]))
        return ccD.n

    def _scale_path(self, clip_bc, g_gmax, extra_mul, ps_small):
        e, tP, vP, sP = self.env, self.tP, self.vP, self.sP
        e.wait("scalar", self.initS, self.initS.n)
        e.wait("scalar", vP, vP.n)
        e.op("scalar", "activation", out=self.sig_t[:], in_=clip_bc[:], func=AF.Sigmoid,
             inc=(sP, 1))
        e.wait("vector", sP, sP.n)
        e.wait("vector", self.ccD, g_gmax)
        e.op("vector", "scalar_tensor_tensor", out=self.scale_t[:], in0=self.sig_t[:],
             scalar=1.0 / QMAX, in1=self.gmax_bc[:], op0=ALU.mult, op1=ALU.mult)
        e.drain_inc("vector")
        e.op("vector", "reciprocal", out=self.inv_bc[:], in_=self.scale_t[:])
        e.op("vector", "tensor_scalar", out=self.sc127_bc[:], in0=self.scale_t[:],
             scalar1=float(extra_mul), scalar2=0.0, op0=ALU.mult, op1=ALU.bypass)
        e.drain_inc("vector", vP, 1)
        return vP.n

    def emit_ffn(self, ph, x_src, write_out, ln2_stats):
        cfg, nc, e = self.cfg, self.nc, self.env
        T, D, I, NT, ND, NI = cfg.T, cfg.D, cfg.I, cfg.NT, cfg.ND, cfg.NI
        st = {}
        tP, vP, sP = self.tP, self.vP, self.sP
        xld = e.sem(ph + "xld")
        wS = e.sem(ph + "wS")
        wdS = e.sem(ph + "wdS")
        hrS = e.sem(ph + "hrS")
        hpS = e.sem(ph + "hpS")
        xoS = e.sem(ph + "xoS")
        pes = ExitStack()
        s, p = self.sb, self.ps

        qwdT = s(pes, ph + "qwdT", [128, NI, D], BF16)
        ln_b = [s(pes, ph + f"ln{i}", [128, D]) for i in range(2)]
        qb_b = [s(pes, ph + f"qb{i}", [128, max(D, I)], BF16) for i in range(2)]
        xw_b = [s(pes, ph + f"xw{i}", [128, D]) for i in range(2)]
        sq_scr = s(pes, ph + "sq_scr", [128, D])
        rowvec = s(pes, ph + "rowvec", [1, D])
        ROWSCALE = s(pes, ph + "ROWSCALE", [128, D])
        ps_small = p(pes, ph + "ps_small", [128, 128])
        ps_tp = [es2.enter_context(nc.psum_tensor(ph + f"ps_tp{i}", [128, 128], BF16))
                 for i, es2 in ((0, pes), (1, pes))]
        tp_guard = [None, None]

        xr = x_src.rearrange("(tt p) d -> p tt d", p=128)

        def ln_stat_finish(rstd_t, negmr_t):
            e.drain_inc("vector")
            e.op("vector", "tensor_scalar", out=self.mean16[:], in0=self.sums[:],
                 scalar1=1.0 / D, scalar2=0.0, op0=ALU.mult, op1=ALU.bypass)
            e.op("vector", "tensor_scalar", out=self.sumsq[:], in0=self.sumsq[:],
                 scalar1=1.0 / D, scalar2=0.0, op0=ALU.mult, op1=ALU.bypass)
            e.drain_inc("vector")
            e.op("vector", "tensor_tensor", out=self.msq16[:], in0=self.mean16[:],
                 in1=self.mean16[:], op=ALU.mult)
            e.drain_inc("vector")
            e.op("vector", "tensor_tensor", out=self.sumsq[:], in0=self.sumsq[:],
                 in1=self.msq16[:], op=ALU.subtract)
            e.drain_inc("vector", vP, 1)
            e.wait("scalar", vP, vP.n)
            e.op("scalar", "activation", out=self.stA[:], in_=self.sumsq[:],
                 func=AF.Sqrt, bias=self.cfg.eps, inc=(sP, 1))
            e.wait("vector", sP, sP.n)
            e.op("vector", "reciprocal", out=rstd_t[:], in_=self.stA[:])
            e.drain_inc("vector")
            e.op("vector", "scalar_tensor_tensor", out=negmr_t[:], in0=self.mean16[:],
                 scalar=-1.0, in1=rstd_t[:], op0=ALU.mult, op1=ALU.mult)
            e.drain_inc("vector", vP, 1)
            return vP.n

        for tt in range(NT):
            b = tt % 2
            if tt >= 2:
                e.wait("sync", vP, st["st_red%d" % (tt - 2)])
            e.dma("sync", xld, out=xw_b[b][:], in_=xr[:, tt, :])
            e.wait("scalar", xld, xld.n)
            if tt >= 2:
                e.wait("scalar", vP, st["sq_red%d" % (tt - 2)])
            e.op("scalar", "activation", out=ln_b[b][:], in_=xw_b[b][:], func=AF.Square,
                 inc=(sP, 1))
            st["p1a_sq%d" % tt] = sP.n
            e.wait("vector", xld, xld.n)
            e.wait("vector", sP, sP.n)
            e.op("vector", "reduce_sum", out=self.sumsq[:, tt:tt + 1], in_=ln_b[b][:], axis=AX.X,
                 inc=(vP, 1))
            st["sq_red%d" % tt] = vP.n
            e.op("vector", "reduce_sum", out=self.sums[:, tt:tt + 1], in_=xw_b[b][:], axis=AX.X,
                 inc=(vP, 1))
            st["st_red%d" % tt] = vP.n
        v_st1 = ln_stat_finish(self.rstd, self.negmr)

        e.wait("scalar", vP, v_st1)
        for tt in range(NT):
            b = tt % 2
            if tt >= 2:
                e.wait("sync", vP, st["lnmax%d" % (tt - 2)])
            else:
                e.wait("sync", vP, st["st_red%d" % (NT - 1)])
                e.wait("sync", sP, st["p1a_sq%d" % (NT - 1)])
            e.dma("sync", xld, out=xw_b[b][:], in_=xr[:, tt, :])
            e.wait("scalar", xld, xld.n)
            if tt >= 2:
                e.wait("scalar", vP, st["lnmax%d" % (tt - 2)])
            e.op("scalar", "activation", out=ln_b[b][:], in_=xw_b[b][:], func=AF.Identity,
                 scale=self.rstd[:, tt:tt + 1], bias=self.negmr[:, tt:tt + 1], inc=(sP, 1))
            e.wait("vector", sP, sP.n)
            e.op("vector", "tensor_reduce", out=self.permax[:, tt:tt + 1], in_=ln_b[b][:], axis=AX.X,
                 op=ALU.max, apply_absolute_value=True, inc=(vP, 1))
            st["lnmax%d" % tt] = vP.n
        e.drain_inc("vector")
        e.op("vector", "tensor_reduce", out=self.lmax1[:], in_=self.permax[:], axis=AX.X, op=ALU.max)
        e.drain_inc("vector", vP, 1)
        g_gmax = self._fold_and_allreduce(ps_small)
        v_sc = self._scale_path(self.clipx_bc, g_gmax, 1.0 / QMAX, ps_small)
        e.op("vector", "tensor_scalar", out=self.stA[:], in0=self.rstd[:], scalar1=self.inv_bc[:, 0:1],
             scalar2=0.0, op0=ALU.mult, op1=ALU.bypass)
        e.op("vector", "tensor_scalar", out=self.stB[:], in0=self.negmr[:], scalar1=self.inv_bc[:, 0:1],
             scalar2=0.0, op0=ALU.mult, op1=ALU.bypass)
        e.drain_inc("vector", vP, 1)
        v_ab = vP.n

        sub1 = ExitStack()
        qxT = s(sub1, ph + "qxT", [128, ND, T], BF16)
        e.wait("scalar", vP, v_ab)
        for tt in range(NT):
            b = tt % 2
            if tt >= 2:
                e.wait("sync", sP, st["qx_act%d" % (tt - 2)])
            else:
                e.wait("sync", vP, st["lnmax%d" % (NT - 1)])
            e.dma("sync", xld, out=xw_b[b][:], in_=xr[:, tt, :])
            e.wait("scalar", xld, xld.n)
            if tt >= 2:
                e.wait("scalar", vP, st["qa_done%d" % (tt - 2)])
                e.wait("scalar", tP, st["qx_tr%d" % (tt - 2)])
            e.op("scalar", "activation", out=ln_b[b][:], in_=xw_b[b][:], func=AF.Identity,
                 scale=self.stA[:, tt:tt + 1], bias=self.stB[:, tt:tt + 1], inc=(sP, 1))
            st["qx_act%d" % tt] = sP.n
            e.wait("vector", sP, sP.n)
            e.op("vector", "tensor_scalar", out=ln_b[b][:], in0=ln_b[b][:], scalar1=MAGIC,
                 scalar2=MAGIC, op0=ALU.add, op1=ALU.subtract)
            e.op("vector", "tensor_scalar", out=qb_b[b][:, 0:D], in0=ln_b[b][:], scalar1=NEG_QMIN,
                 scalar2=QMAX, op0=ALU.max, op1=ALU.min, inc=(vP, 1))
            st["qa_done%d" % tt] = vP.n
            e.wait("tensor", vP, vP.n)
            for dc in range(ND):
                pb = dc % 2
                e.wait_g("tensor", tp_guard[pb])
                e.op("tensor", "transpose", out=ps_tp[pb][:], in_=qb_b[b][:, dc * 128:(dc + 1) * 128],
                     identity=self.identB[:], inc=(tP, 1))
                st["qx_tr%d" % tt] = tP.n
                e.wait("scalar", tP, tP.n)
                e.op("scalar", "copy", out=qxT[:, dc, tt * 128:(tt + 1) * 128], in_=ps_tp[pb][:],
                     inc=(sP, 1))
                tp_guard[pb] = (sP, sP.n)
        s_qx = sP.n
        wg_b = [s(sub1, ph + f"wg{i}", [128, D]) for i in range(2)]
        wu_b = [s(sub1, ph + f"wu{i}", [128, D]) for i in range(2)]
        qwg_b = [s(sub1, ph + f"qwg{i}", [128, ND, 128], BF16) for i in range(2)]
        qwu_b = [s(sub1, ph + f"qwu{i}", [128, ND, 128], BF16) for i in range(2)]
        wt1 = s(sub1, ph + "wt1", [128, max(D, I)])
        coefg = s(sub1, ph + "coefg", [128, 2])
        coefu = s(sub1, ph + "coefu", [128, 2])
        rmax_w = s(sub1, ph + "rmax_w", [128, 2])
        wrec = s(sub1, ph + "wrec", [128, 1])
        silu_b = [s(sub1, ph + f"silu{i}", [128, 512]) for i in range(2)]
        hpre_b = [s(sub1, ph + f"hpre{i}", [128, 1024]) for i in range(2)]
        NHM = NI * (T // 512)
        hmax_arr = s(sub1, ph + "hmax_arr", [128, NHM])
        ps_g = [p(sub1, ph + f"psg{i}", [128, 512]) for i in range(2)]
        ps_u = [p(sub1, ph + f"psu{i}", [128, 512]) for i in range(2)]

        def quant_weight(wraw, wq_out, qwT, coef, par, have_coef, nd):
            e.op("vector", "tensor_reduce", out=rmax_w[:, par:par + 1], in_=wraw[:], axis=AX.X,
                 op=ALU.max, apply_absolute_value=True)
            e.drain_inc("vector")
            e.op("vector", "reciprocal", out=wrec[:], in_=rmax_w[:, par:par + 1])
            e.drain_inc("vector")
            e.op("vector", "tensor_scalar", out=wrec[:], in0=wrec[:], scalar1=QMAX, scalar2=0.0,
                 op0=ALU.mult, op1=ALU.bypass)
            if have_coef:
                e.op("vector", "tensor_tensor", out=coef[:, par:par + 1], in0=rmax_w[:, par:par + 1],
                     in1=self.sc127_bc[:, 0:1], op=ALU.mult)
            e.drain_inc("vector", vP, 1)
            e.wait("scalar", vP, vP.n)
            e.wait("scalar", tP, tP.n)
            e.op("scalar", "activation", out=wq_out[:, 0:nd * 128], in_=wraw[:], func=AF.Identity,
                 scale=wrec[:, 0:1], bias=self.cC[:, 0:1], inc=(sP, 1))
            e.wait("vector", sP, sP.n)
            e.op("vector", "tensor_scalar", out=wq_out[:, 0:nd * 128], in0=wq_out[:, 0:nd * 128],
                 scalar1=MAGIC, scalar2=NEG_QMIN, op0=ALU.subtract, op1=ALU.max)
            e.op("vector", "tensor_scalar", out=qb_b[0][:, 0:nd * 128], in0=wq_out[:, 0:nd * 128],
                 scalar1=QMAX, scalar2=0.0, op0=ALU.min, op1=ALU.bypass, inc=(vP, 1))
            e.wait("tensor", vP, vP.n)
            for dc in range(nd):
                pb = dc % 2
                e.wait_g("tensor", tp_guard[pb])
                e.op("tensor", "transpose", out=ps_tp[pb][:],
                     in_=qb_b[0][:, dc * 128:(dc + 1) * 128],
                     identity=self.identB[:], inc=(tP, 1))
                e.wait("scalar", tP, tP.n)
                e.op("scalar", "copy", out=qwT[:, dc, :],
                     in_=ps_tp[pb][:], inc=(sP, 1))
                tp_guard[pb] = (sP, sP.n)
            return sP.n

        NTC = T // 512
        for it in range(NI):
            b = it % 2
            if it >= 2:
                e.wait("sync", sP, st["wq_done%d" % (it - 2)])
            e.dma("sync", wS, out=wg_b[b][:], in_=self.w["wg"][it * 128:(it + 1) * 128, :])
            e.dma("sync", wS, out=wu_b[b][:], in_=self.w["wu"][it * 128:(it + 1) * 128, :])
            ld_w = wS.n
            e.wait("vector", wS, ld_w)
            e.wait("vector", vP, v_sc)
            if it >= 2:
                e.wait("scalar", tP, st["mm_done%d" % (it - 2)])
                e.wait("vector", sP, st["silu_done%d" % (it - 2)])
            quant_weight(wg_b[b], wt1, qwg_b[b], coefg, b, True, ND)
            s_wq = quant_weight(wu_b[b], wt1, qwu_b[b], coefu, b, True, ND)
            st["wq_done%d" % it] = s_wq
            e.wait("tensor", sP, s_wq)
            if it == 0:
                e.wait("tensor", sP, s_qx)
            for tc in range(NTC):
                pb = tc % 2
                k = it * NTC + tc
                if k >= 2:
                    e.wait("tensor", vP, st["gu_ev%d" % (k - 2)])
                for dc in range(ND):
                    e.op("tensor", "matmul", out=ps_g[pb][:], lhsT=qwg_b[b][:, dc, :],
                         rhs=qxT[:, dc, tc * 512:(tc + 1) * 512],
                         start=(dc == 0), stop=(dc == ND - 1))
                for dc in range(ND):
                    e.op("tensor", "matmul", out=ps_u[pb][:], lhsT=qwu_b[b][:, dc, :],
                         rhs=qxT[:, dc, tc * 512:(tc + 1) * 512],
                         start=(dc == 0), stop=(dc == ND - 1),
                         inc=(tP, 1) if dc == ND - 1 else None)
                st["mm_done%d" % it] = tP.n
                e.wait("scalar", tP, tP.n)
                e.op("scalar", "activation", out=silu_b[pb][:], in_=ps_g[pb][:], func=AF.Silu,
                     scale=coefg[:, b:b + 1], inc=(sP, 1))
                st["silu_done%d" % it] = sP.n
                hb = (tc // 2) % 2
                half = tc % 2
                e.wait("vector", sP, sP.n)
                if "hpre_dma%d" % hb in st:
                    e.wait("vector", hpS, st["hpre_dma%d" % hb])
                e.op("vector", "scalar_tensor_tensor",
                     out=hpre_b[hb][:, half * 512:half * 512 + 512],
                     in0=ps_u[pb][:], scalar=coefu[:, b:b + 1], in1=silu_b[pb][:],
                     op0=ALU.mult, op1=ALU.mult)
                e.op("vector", "tensor_reduce", out=hmax_arr[:, k:k + 1],
                     in_=hpre_b[hb][:, half * 512:half * 512 + 512], axis=AX.X,
                     op=ALU.max, apply_absolute_value=True, inc=(vP, 1))
                st["gu_ev%d" % k] = vP.n
                if half == 1:
                    e.wait("gpsimd", vP, vP.n)
                    e.dma("gpsimd", hpS,
                          out=self.hpre_scr.rearrange("(ic p) t -> p ic t", p=128)[
                              :, it, (tc // 2) * 1024:(tc // 2) * 1024 + 1024],
                          in_=hpre_b[hb][:])
                    st["hpre_dma%d" % hb] = hpS.n
        v_gu = vP.n
        g_hpre = hpS.n

        e.wait("vector", vP, v_gu)
        e.drain_inc("vector")
        e.op("vector", "tensor_reduce", out=self.lmax1[:], in_=hmax_arr[:], axis=AX.X, op=ALU.max)
        e.drain_inc("vector", vP, 1)
        g_gmax2 = self._fold_and_allreduce(ps_small)
        e.wait("tensor", vP, vP.n)
        wd_b = [s(sub1, ph + "wd0", [128, I])]
        NHC = D // 128
        for hc in range(NHC):
            b = 0
            if hc >= 1:
                e.wait("sync", sP, st["wd_done%d" % (hc - 1)])
            e.dma("sync", wdS, out=wd_b[b][:], in_=self.w["wd"][hc * 128:(hc + 1) * 128, :])
            e.wait("vector", wdS, wdS.n)
            if hc >= 1:
                e.wait("vector", tP, st["rv_tr%d" % (hc - 1)])
            e.op("vector", "tensor_reduce", out=rmax_w[:, 0:1], in_=wd_b[b][:], axis=AX.X,
                 op=ALU.max, apply_absolute_value=True)
            e.drain_inc("vector")
            e.op("vector", "reciprocal", out=wrec[:], in_=rmax_w[:, 0:1])
            e.drain_inc("vector")
            e.op("vector", "tensor_scalar", out=wrec[:], in0=wrec[:], scalar1=QMAX, scalar2=0.0,
                 op0=ALU.mult, op1=ALU.bypass)
            e.drain_inc("vector", vP, 1)
            v_rm = vP.n
            e.wait("tensor", vP, v_rm)
            e.op("tensor", "transpose", out=ps_small[0:1, :], in_=rmax_w[:, 0:1],
                 identity=self.identF[:], inc=(tP, 1))
            st["rv_tr%d" % hc] = tP.n
            e.wait("vector", tP, tP.n)
            e.op("vector", "tensor_copy", out=rowvec[0:1, hc * 128:(hc + 1) * 128],
                 in_=ps_small[0:1, :], inc=(vP, 1))
            e.wait("scalar", vP, v_rm)
            e.wait("scalar", tP, tP.n)
            e.op("scalar", "activation", out=wt1[:, 0:I], in_=wd_b[b][:], func=AF.Identity,
                 scale=wrec[:, 0:1], bias=self.cC[:, 0:1], inc=(sP, 1))
            e.wait("vector", sP, sP.n)
            e.op("vector", "tensor_scalar", out=wt1[:, 0:I], in0=wt1[:, 0:I], scalar1=MAGIC,
                 scalar2=NEG_QMIN, op0=ALU.subtract, op1=ALU.max)
            e.op("vector", "tensor_scalar", out=qb_b[1][:, 0:I], in0=wt1[:, 0:I], scalar1=QMAX,
                 scalar2=0.0, op0=ALU.min, op1=ALU.bypass, inc=(vP, 1))
            e.wait("tensor", vP, vP.n)
            for ic in range(NI):
                pb = ic % 2
                e.wait_g("tensor", tp_guard[pb])
                e.op("tensor", "transpose", out=ps_tp[pb][:], in_=qb_b[1][:, ic * 128:(ic + 1) * 128],
                     identity=self.identB[:], inc=(tP, 1))
                e.wait("scalar", tP, tP.n)
                e.op("scalar", "copy", out=qwdT[:, ic, hc * 128:(hc + 1) * 128], in_=ps_tp[pb][:],
                     inc=(sP, 1))
                tp_guard[pb] = (sP, sP.n)
            st["wd_done%d" % hc] = sP.n
        s_wd = sP.n
        v_sch = self._scale_path(self.clipd_bc, g_gmax2, 0.5 / QMAX, ps_small)


        e.wait("vector", sP, s_wd)
        e.wait("vector", vP, v_sch)
        e.drain_inc("vector")
        e.op("vector", "tensor_scalar", out=rowvec[:], in0=rowvec[:],
             scalar1=self.sc127_bc[0:1, 0:1], scalar2=0.0, op0=ALU.mult, op1=ALU.bypass)
        e.drain_inc("vector", vP, 1)
        v_rv = vP.n
        sub1.close()
        sub2 = ExitStack()
        NDC = D // 512
        NIH = (NI + 1) // 2
        ps_d = [p(sub2, ph + f"psd{i}", [128, 512]) for i in range(2 * NDC)]
        hraw_b = [s(sub2, ph + f"hraw{i}", [128, NIH, 512]) for i in range(2)]
        qh_b = [s(sub2, ph + f"qh{i}", [128, NI, 512], BF16) for i in range(2)]
        hq1 = s(sub2, ph + "hq1", [128, 512])
        hq1b = s(sub2, ph + "hq1b", [128, 512])
        hqs = [hq1, hq1b]
        evt = s(sub2, ph + "evt", [128, 512])
        xo_b = [s(sub2, ph + f"xo{i}", [128, D]) for i in range(2)]

        e.wait("tensor", vP, v_rv)
        for c in range(NDC):
            e.op("tensor", "matmul", out=ps_d[c][:], lhsT=self.ones_r[:],
                 rhs=rowvec[0:1, c * 512:(c + 1) * 512], start=True, stop=True,
                 inc=(tP, 1) if c == NDC - 1 else None)
        e.wait("vector", tP, tP.n)
        for c in range(NDC):
            e.op("vector", "tensor_copy", out=ROWSCALE[:, c * 512:(c + 1) * 512],
                 in_=ps_d[c][:], inc=(vP, 1) if c == NDC - 1 else None)
        v_rs = vP.n

        hsr = self.hpre_scr.rearrange("(ic p) t -> p ic t", p=128)
        NTC2 = T // 512
        v_invh = v_sch
        for tc in range(NTC2):
            b = tc % 2
            for j in range(2):
                lo = j * NIH
                hi = min(NI, lo + NIH)
                if tc >= 2:
                    e.wait("sync", sP, st["hq_h%d_%d" % (tc - 2, j)])
                if tc == 0 and j == 0:
                    e.wait("sync", hpS, g_hpre)
                e.dma("sync", hrS, out=hraw_b[j][:, 0:hi - lo, :],
                      in_=hsr[:, lo:hi, tc * 512:(tc + 1) * 512])
                st["hld%d_%d" % (tc, j)] = hrS.n
            e.wait("scalar", vP, v_invh)
            e.wait("scalar", vP, vP.n)
            if tc >= 2:
                e.wait("scalar", tP, st["dmm%d" % (tc - 2)])
            for ic in range(NI):
                j = ic // NIH
                hq = hqs[ic % 2]
                if ic % NIH == 0:
                    e.wait("scalar", hrS, st["hld%d_%d" % (tc, j)])
                if ic >= 2:
                    e.wait("scalar", vP, st["hqv%d" % (ic - 2)])
                e.op("scalar", "activation", out=hq[:], in_=hraw_b[j][:, ic - j * NIH, :],
                     func=AF.Identity, scale=self.inv_bc[:, 0:1], bias=self.cC[:, 0:1],
                     inc=(sP, 1))
                if ic == min(NI, NIH) - 1:
                    st["hq_h%d_0" % tc] = sP.n
                e.wait("vector", sP, sP.n)
                e.op("vector", "tensor_scalar", out=hq[:], in0=hq[:], scalar1=MAGIC,
                     scalar2=NEG_QMIN, op0=ALU.subtract, op1=ALU.max)
                e.op("vector", "tensor_scalar", out=qh_b[b][:, ic, :], in0=hq[:], scalar1=QMAX,
                     scalar2=0.0, op0=ALU.min, op1=ALU.bypass, inc=(vP, 1))
                st["hqv%d" % ic] = vP.n
            st["hq_h%d_1" % tc] = sP.n
            v_qh = vP.n
            e.wait("tensor", vP, v_qh)
            if tc == 0:
                e.wait("tensor", sP, s_wd)
            for tl in range(4):
                tt = tc * 4 + tl
                bo = tl % 2
                pd = [ps_d[(tl % 2) * NDC + c] for c in range(NDC)]
                if tt >= 2:
                    e.wait("tensor", vP, st["d_ev%d" % (tt - 2)])
                for ic in range(NI):
                    for c in range(NDC):
                        e.op("tensor", "matmul", out=pd[c][:],
                             lhsT=qh_b[b][:, ic, tl * 128:(tl + 1) * 128],
                             rhs=qwdT[:, ic, c * 512:(c + 1) * 512],
                             start=(ic == 0), stop=(ic == NI - 1),
                             inc=(tP, 1) if (ic == NI - 1 and c == NDC - 1) else None)
                st["dmm%d" % tc] = tP.n
                if tt >= 2:
                    e.wait("sync", vP, st["d_ev%d" % (tt - 2)])
                else:
                    e.wait("sync", sP, st["qx_act%d" % (NT - 1)])
                e.dma("sync", xld, out=xw_b[bo][:], in_=xr[:, tt, :])
                ld_x = xld.n
                e.wait("vector", tP, tP.n)
                e.wait("vector", vP, v_rs)
                e.wait("vector", xld, ld_x)
                e.wait("vector", sP, sP.n)
                if tt >= 2:
                    e.wait("vector", xoS, st["xo_dma%d" % (tt - 2)])
                for c in range(NDC):
                    e.op("vector", "tensor_tensor", out=evt[:], in0=pd[c][:],
                         in1=ROWSCALE[:, c * 512:(c + 1) * 512], op=ALU.mult)
                    e.op("vector", "tensor_tensor", out=xo_b[bo][:, c * 512:(c + 1) * 512],
                         in0=xw_b[bo][:, c * 512:(c + 1) * 512], in1=evt[:],
                         op=ALU.add, inc=(vP, 1) if c == NDC - 1 else None)
                st["d_ev%d" % tt] = vP.n
                if ln2_stats:
                    e.wait("scalar", vP, vP.n)
                    if tt >= 2:
                        e.wait("scalar", vP, st["sq2_red%d" % (tt - 2)])
                    e.op("scalar", "activation", out=hq1[:], in_=xo_b[bo][:, 0:512],
                         func=AF.Square, inc=(sP, 1))
                    e.op("scalar", "activation", out=evt[:], in_=xo_b[bo][:, 512:D],
                         func=AF.Square, inc=(sP, 1)) if D > 512 else None
                    e.wait("vector", sP, sP.n)
                    e.op("vector", "reduce_sum", out=self.sumsq[:, tt:tt + 1], in_=hq1[:],
                         axis=AX.X)
                    if D > 512:
                        e.op("vector", "reduce_sum", out=self.msq16[:, 0:1], in_=evt[:], axis=AX.X)
                        e.drain_inc("vector")
                        e.op("vector", "tensor_tensor", out=self.sumsq[:, tt:tt + 1],
                             in0=self.sumsq[:, tt:tt + 1], in1=self.msq16[:, 0:1], op=ALU.add)
                    e.drain_inc("vector", vP, 1)
                    st["sq2_red%d" % tt] = vP.n
                    e.op("vector", "reduce_sum", out=self.sums[:, tt:tt + 1], in_=xo_b[bo][:],
                         axis=AX.X, inc=(vP, 1))
                e.wait("gpsimd", vP, vP.n)
                e.dma("gpsimd", xoS,
                      out=write_out.rearrange("(tt p) d -> p tt d", p=128)[:, tt, :],
                      in_=xo_b[bo][:])
                st["xo_dma%d" % tt] = xoS.n
        if ln2_stats:
            ln_stat_finish(self.rstd2, self.negmr2)
        e.wait("gpsimd", xoS, xoS.n)
        self.env.barrier(self.bar)
        sub2.close()
        pes.close()

    def emit_attn(self):
        cfg, nc, e = self.cfg, self.nc, self.env
        T, D, H, DH, NT, ND = cfg.T, cfg.D, cfg.H, cfg.DH, cfg.NT, cfg.ND
        NH2 = DH // 2
        NDC = D // 512
        HPC = 512 // DH
        st = {}
        tP, vP, sP = self.tP, self.vP, self.sP
        e2 = self.env
        axS = e2.sem("axS")
        awS = e2.sem("awS")
        qrS = e2.sem("qrS")
        qcS = e2.sem("qcS")
        x2S = e2.sem("x2S")
        pes = ExitStack()
        s, p = self.sb, self.ps

        KrT = s(pes, "KrT", [128, ND, T])
        VS = 66
        Varena = s(pes, "Varena", [128, NT, H * VS])
        WT = s(pes, "WT", [128, ND, D])
        sub_pj = ExitStack()
        cosT = s(sub_pj, "cosT", [128, NT, DH // 2])
        sinT = s(sub_pj, "sinT", [128, NT, DH // 2])
        csS = e2.sem("csS")
        e.dma("sync", csS, out=cosT[:], in_=self.cos_d.rearrange("p t i -> p (t i)"))
        e.dma("sync", csS, out=sinT[:], in_=self.sin_d.rearrange("p t i -> p (t i)"))
        cs_ld = csS.n
        lnT = s(sub_pj, "lnT", [128, ND, 128])
        x1_b = [s(sub_pj, f"ax1_{i}", [128, D]) for i in range(2)]
        ln2_b = [s(sub_pj, f"aln_{i}", [128, D]) for i in range(2)]
        pr_b = ln2_b
        rp_t = s(sub_pj, "rp_t", [128, 512])
        w_b = [s(sub_pj, f"aw_{i}", [128, D]) for i in range(2)]
        ps_pj = [p(sub_pj, "ps_pj%d" % i, [128, 512]) for i in range(2)]
        ps_tp = [p(sub_pj, "ps_tp%d" % i, [128, 128]) for i in range(2)]
        tp_guard = [None, None]
        x2pS = e2.sem("x2pS")
        e.dma("gpsimd", x2pS, out=self.x2_scr, in_=self.x1_scr)

        e.op("vector", "memset",
             ap=Varena[:].rearrange("p t (h s) -> p t h s", h=H)[:, :, :, 64:65],
             constant=1.0, inc=(vP, 1))
        def prep_WT(w_ap):
            e.wait("scalar", tP, tP.n)
            for oc in range(ND):
                b = oc % 2
                if oc >= 2:
                    e.wait("sync", tP, st["aWtr%d" % (oc - 2)])
                e.dma("sync", awS, out=w_b[b][:], in_=w_ap[oc * 128:(oc + 1) * 128, :])
                e.wait("tensor", awS, awS.n)
                for dcc in range(ND):
                    pb = dcc % 2
                    e.wait_g("tensor", tp_guard[pb])
                    e.op("tensor", "transpose", out=ps_tp[pb][:],
                         in_=w_b[b][:, dcc * 128:(dcc + 1) * 128], identity=self.identF[:],
                         inc=(tP, 1))
                    st["aWtr%d" % oc] = tP.n
                    e.wait("scalar", tP, tP.n)
                    e.op("scalar", "copy", out=WT[:, dcc, oc * 128:(oc + 1) * 128], in_=ps_tp[pb][:],
                         inc=(sP, 1))
                    tp_guard[pb] = (sP, sP.n)
            return sP.n

        def pass_proj(w_ap, kind):
            s_w = prep_WT(w_ap)
            for tt in range(NT):
                b = tt % 2
                if tt >= 2:
                    e.wait("sync", sP, st["aln_done%d" % (tt - 2)])
                e.dma("sync", axS, out=x1_b[b][:],
                      in_=self.x1_scr.rearrange("(tt p) d -> p tt d", p=128)[:, tt, :])
                e.wait("scalar", axS, axS.n)
                e.wait("scalar", tP, tP.n)
                e.op("scalar", "activation", out=ln2_b[b][:], in_=x1_b[b][:], func=AF.Identity,
                     scale=self.rstd2[:, tt:tt + 1], bias=self.negmr2[:, tt:tt + 1], inc=(sP, 1))
                st["aln_done%d" % tt] = sP.n
                e.wait("tensor", sP, sP.n)
                for dc in range(ND):
                    pb = dc % 2
                    e.wait_g("tensor", tp_guard[pb])
                    e.op("tensor", "transpose", out=ps_tp[pb][:],
                         in_=ln2_b[b][:, dc * 128:(dc + 1) * 128], identity=self.identF[:],
                         inc=(tP, 1))
                    e.wait("vector", tP, tP.n)
                    if kind == "Q" and "aQdma%d" % (tt - 1) in st:
                        e.wait("vector", qrS, st["aQdma%d" % (tt - 1)])
                    e.op("vector", "tensor_copy", out=lnT[:, dc, :], in_=ps_tp[pb][:],
                         inc=(vP, 1))
                    tp_guard[pb] = (vP, vP.n)
                v_lnT = vP.n
                e.wait("tensor", vP, v_lnT)
                if tt == 0:
                    e.wait("tensor", sP, s_w)
                for c in range(NDC):
                    for dc in range(ND):
                        e.op("tensor", "matmul", out=ps_pj[c][:], lhsT=lnT[:, dc, :],
                             rhs=WT[:, dc, c * 512:(c + 1) * 512],
                             start=(dc == 0), stop=(dc == ND - 1),
                             inc=(tP, 1) if dc == ND - 1 else None)
                mm_n = tP.n
                if kind == "V":
                    e.wait("vector", tP, mm_n)
                    for h in range(H):
                        c = (h * 64) // 512
                        off = (h * 64) % 512
                        e.op("vector", "tensor_copy",
                             out=Varena[:, tt, h * VS:h * VS + 64],
                             in_=ps_pj[c][:, off:off + 64],
                             inc=(vP, 1) if h == H - 1 else None)
                else:
                    e.wait("vector", tP, mm_n)
                    if tt >= 2:
                        e.wait("vector", tP, st["aRtr%d" % (tt - 2)])
                    e.wait("vector", csS, cs_ld)
                    cosb = cosT[:, tt, :].unsqueeze(1).broadcast_to([128, HPC, NH2])
                    sinb = sinT[:, tt, :].unsqueeze(1).broadcast_to([128, HPC, NH2])
                    for c in range(NDC):
                        pj = ps_pj[c][:].rearrange("p (h i two) -> p h i two", h=HPC, two=2)
                        pr = pr_b[b][:, c * 512:(c + 1) * 512].rearrange(
                            "p (h i two) -> p h i two", h=HPC, two=2)
                        rp = rp_t[:, 0:HPC * NH2].rearrange("p (h i) -> p h i", h=HPC)
                        e.op("vector", "tensor_tensor", out=pr[:, :, :, 0], in0=pj[:, :, :, 0],
                             in1=cosb, op=ALU.mult)
                        e.op("vector", "tensor_tensor", out=rp, in0=pj[:, :, :, 1], in1=sinb,
                             op=ALU.mult)
                        e.op("vector", "tensor_tensor", out=pr[:, :, :, 0], in0=pr[:, :, :, 0],
                             in1=rp, op=ALU.subtract)
                        e.op("vector", "tensor_tensor", out=pr[:, :, :, 1], in0=pj[:, :, :, 0],
                             in1=sinb, op=ALU.mult)
                        e.op("vector", "tensor_tensor", out=rp, in0=pj[:, :, :, 1], in1=cosb,
                             op=ALU.mult)
                        e.op("vector", "tensor_tensor", out=pr[:, :, :, 1], in0=pr[:, :, :, 1],
                             in1=rp, op=ALU.add, inc=(vP, 1))
                    v_rope = vP.n
                    e.wait("tensor", vP, v_rope)
                    for dc in range(ND):
                        pb = dc % 2
                        e.wait_g("tensor", tp_guard[pb])
                        e.op("tensor", "transpose", out=ps_tp[pb][:],
                             in_=pr_b[b][:, dc * 128:(dc + 1) * 128], identity=self.identF[:],
                             inc=(tP, 1))
                        st["aRtr%d" % tt] = tP.n
                        e.wait("scalar", tP, tP.n)
                        if kind == "K":
                            e.op("scalar", "copy", out=KrT[:, dc, tt * 128:(tt + 1) * 128],
                                 in_=ps_tp[pb][:], inc=(sP, 1))
                        else:
                            if dc == 0 and "aQdma%d" % (tt - 1) in st:
                                e.wait("scalar", qrS, st["aQdma%d" % (tt - 1)])
                            e.op("scalar", "copy", out=lnT[:, dc, :], in_=ps_tp[pb][:], inc=(sP, 1))
                        tp_guard[pb] = (sP, sP.n)
                    if kind == "Q":
                        e.wait("gpsimd", sP, sP.n)
                        e.dma("gpsimd", qrS,
                              out=self.qr_scr.rearrange("(dc p) t -> p dc t", p=128)[
                                  :, :, tt * 128:(tt + 1) * 128],
                              in_=lnT[:])
                        st["aQdma%d" % tt] = qrS.n

        pass_proj(self.w["wk"], "K")
        s_kr = sP.n
        pass_proj(self.w["wv"], "V")
        v_kv = vP.n
        pass_proj(self.w["wq"], "Q")
        g_qr = qrS.n
        s_wo = prep_WT(self.w["wo"])
        sub_pj.close()

        QrC = s(pes, "QrC", [128, ND, 512])
        E_b = [s(pes, f"aE{i}", [128, 512]) for i in range(2)]
        Dbc = s(pes, "Dbc", [64, 512])
        dinv = Dbc[0:1, :]
        headsN = s(pes, "headsN", [128, ND, 512])
        x2_b = [s(pes, "ax2s", [128, D])] * 2
        ps_s = [p(pes, "ps_s%d" % i, [128, 512]) for i in range(2)]
        ps_av = [p(pes, "ps_av%d" % i, [128, 512]) for i in range(2)]
        ps_o = [p(pes, "ps_o%d" % i, [128, 512]) for i in range(2)]

        g_ps_s = [None, None]
        g_ps_av = [None, None]
        g_ps_o = [None, None]
        g_E = [None, None]
        g_x2 = [None, None]

        NQC = T // 512
        e.wait("tensor", sP, s_wo)
        e.wait("tensor", vP, v_kv)
        for qc in range(NQC):
            if qc >= 1:
                e.wait("sync", tP, st["aQcons%d" % (qc - 1)])
            e.wait("sync", qrS, g_qr)
            e.dma("sync", qcS, out=QrC[:],
                  in_=self.qr_scr.rearrange("(dc p) t -> p dc t", p=128)[
                      :, :, qc * 512:(qc + 1) * 512])
            ld_q = qcS.n
            e.wait("tensor", qcS, ld_q)
            nkt = qc * 4 + 4
            for h in range(H):
                dc = h // 2
                pb2 = (h % 2) * 64
                avp = ps_av[h % 2]
                e.wait_g("tensor", g_ps_av[h % 2])

                def emit_S(kt):
                    j = kt - qc * 4
                    q0 = max(0, j * 128)
                    qn = 512 - q0
                    slot = kt % 2
                    e.wait_g("tensor", g_ps_s[slot])
                    e.op("tensor", "matmul", out=ps_s[slot][:, 0:qn],
                         lhsT=KrT[pb2:pb2 + 64, dc, kt * 128:(kt + 1) * 128],
                         rhs=QrC[pb2:pb2 + 64, dc, q0:512],
                         start=True, stop=True, inc=(tP, 1))
                    st["aSmm%d" % slot] = tP.n

                def emit_exp(kt):
                    j = kt - qc * 4
                    q0 = max(0, j * 128)
                    qn = 512 - q0
                    slot = kt % 2
                    e.wait("scalar", tP, st["aSmm%d" % slot])
                    e.wait_g("scalar", g_E[slot])
                    e.op("scalar", "activation", out=E_b[slot][:, 0:qn], in_=ps_s[slot][:, 0:qn],
                         func=AF.Exp, scale=0.125, inc=(sP, 1))
                    g_ps_s[slot] = (sP, sP.n)
                    if j >= 0:
                        e.wait("vector", sP, sP.n)
                        e.op("vector", "tensor_tensor", out=E_b[slot][:, 0:128],
                             in0=E_b[slot][:, 0:128], in1=self.tri[:], op=ALU.mult,
                             inc=(vP, 1))
                        return (vP, vP.n)
                    return (sP, sP.n)

                def emit_AV(kt, e_ready):
                    j = kt - qc * 4
                    q0 = max(0, j * 128)
                    qn = 512 - q0
                    slot = kt % 2
                    e.wait_g("tensor", e_ready)
                    e.op("tensor", "matmul", out=avp[0:65, q0:512],
                         lhsT=Varena[:, kt, h * VS:h * VS + 65],
                         rhs=E_b[slot][:, 0:qn],
                         start=(kt == 0), stop=(kt == nkt - 1), inc=(tP, 1))
                    g_E[slot] = (tP, tP.n)

                emit_S(0)
                er = emit_exp(0)
                for kt in range(nkt - 1):
                    emit_S(kt + 1)
                    er_next_ready = er
                    emit_AV(kt, er_next_ready)
                    er = emit_exp(kt + 1)
                emit_AV(nkt - 1, er)
                e.wait("vector", tP, tP.n)
                e.op("vector", "tensor_copy", out=dinv, in_=avp[64:65, :])
                e.drain_inc("vector")
                e.op("vector", "reciprocal", out=dinv, in_=dinv)
                e.drain_inc("vector", vP, 1)
                e.wait_g("tensor", g_ps_s[0])
                e.wait("tensor", vP, vP.n)
                e.op("tensor", "matmul", out=ps_s[0][0:64, :], lhsT=self.ones_r[0:1, 0:64],
                     rhs=dinv, start=True, stop=True, inc=(tP, 1))
                e.wait("vector", tP, tP.n)
                if qc > 0 and h < 2:
                    e.wait("vector", tP, st["aOmm%d" % (qc - 1)])
                e.op("vector", "tensor_copy", out=Dbc[:], in_=ps_s[0][0:64, :])
                e.op("vector", "tensor_tensor", out=headsN[pb2:pb2 + 64, dc, :],
                     in0=avp[0:64, :], in1=Dbc[:], op=ALU.mult, inc=(vP, 1))
                g_ps_s[0] = (vP, vP.n)
                g_ps_av[h % 2] = (vP, vP.n)
            v_hN = vP.n
            st["aQcons%d" % qc] = tP.n
            e.wait("tensor", vP, v_hN)
            if qc == 0:
                e.wait("tensor", sP, s_wo)
            for tl in range(4):
                tt = qc * 4 + tl
                b = tl % 2
                for c in range(NDC):
                    e.wait_g("tensor", g_ps_o[c])
                    for dcc in range(ND):
                        e.op("tensor", "matmul", out=ps_o[c][:],
                             lhsT=headsN[:, dcc, tl * 128:(tl + 1) * 128],
                             rhs=WT[:, dcc, c * 512:(c + 1) * 512],
                             start=(dcc == 0), stop=(dcc == ND - 1),
                             inc=(tP, 1) if dcc == ND - 1 else None)
                st["aOmm%d" % qc] = tP.n
                mm_o = tP.n
                e.wait("vector", tP, mm_o)
                e.wait_g("vector", g_x2[b])
                for c in range(NDC):
                    e.op("vector", "tensor_copy", out=x2_b[b][:, c * 512:(c + 1) * 512],
                         in_=ps_o[c][:], inc=(vP, 1) if c == NDC - 1 else None)
                st["ax2ev%d" % tt] = vP.n
                g_ps_o[0] = (vP, vP.n)
                g_ps_o[1] = (vP, vP.n)
                e.wait("gpsimd", vP, vP.n)
                if qc == 0 and tl == 0:
                    e.wait("gpsimd", x2pS, x2pS.n)
                e.dma("gpsimd", x2S, accum=True,
                      out=self.x2_scr.rearrange("(tt p) d -> p tt d", p=128)[:, tt, :],
                      in_=x2_b[b][:])
                g_x2[0] = g_x2[1] = (x2S, x2S.n)
        e.wait("gpsimd", x2S, x2S.n)
        self.env.barrier(self.bar)
        pes.close()


def make_inputs(cfg, full_inputs):
    X = np.ascontiguousarray(np.asarray(full_inputs["L0_attn_X"], dtype=np.float32))
    base = {
        "wq": np.ascontiguousarray(np.asarray(full_inputs["L0_attn_W_q"], np.float32)),
        "wk": np.ascontiguousarray(np.asarray(full_inputs["L0_attn_W_k"], np.float32)),
        "wv": np.ascontiguousarray(np.asarray(full_inputs["L0_attn_W_v"], np.float32)),
        "wo": np.ascontiguousarray(np.asarray(full_inputs["L0_attn_W_o"], np.float32)),
        "wg": np.ascontiguousarray(np.asarray(full_inputs["L0_ffn_w_gate"], np.float32)),
        "wu": np.ascontiguousarray(np.asarray(full_inputs["L0_ffn_w_up"], np.float32)),
        "wd": np.ascontiguousarray(np.asarray(full_inputs["L0_ffn_w_down"], np.float32)),
        "clipx": np.asarray(full_inputs["L0_ffn_clip_x"], np.float32).reshape(1, 1),
        "clipd": np.asarray(full_inputs["L0_ffn_clip_down"], np.float32).reshape(1, 1),
    }
    maps = []
    for c in range(cfg.n_cores):
        m = dict(base)
        m["x"] = np.ascontiguousarray(X[c])
        maps.append(m)
    return maps


_CFG = Cfg(T=2048, D=1024, H=16, I=2816, n_cores=8)
_MK_CACHE = {}


def _get_mk():
    if "mk" not in _MK_CACHE:
        _MK_CACHE["mk"] = MK(_CFG, with_attn=True)
    return _MK_CACHE["mk"]


def kernel(L0_attn_X, L0_attn_W_q, L0_attn_W_k, L0_attn_W_v, L0_attn_W_o,
           L0_ffn_clip_x, L0_ffn_clip_down, L0_ffn_w_gate, L0_ffn_w_up, L0_ffn_w_down):
    full = {
        "L0_attn_X": L0_attn_X, "L0_attn_W_q": L0_attn_W_q, "L0_attn_W_k": L0_attn_W_k,
        "L0_attn_W_v": L0_attn_W_v, "L0_attn_W_o": L0_attn_W_o,
        "L0_ffn_clip_x": L0_ffn_clip_x, "L0_ffn_clip_down": L0_ffn_clip_down,
        "L0_ffn_w_gate": L0_ffn_w_gate, "L0_ffn_w_up": L0_ffn_w_up,
        "L0_ffn_w_down": L0_ffn_w_down,
    }
    m = _get_mk()
    in_maps = make_inputs(_CFG, full)
    res = run_bass_kernel_spmd(m.nc, in_maps, core_ids=list(range(_CFG.n_cores)))
    return np.stack([np.asarray(res.results[i]["out"]) for i in range(_CFG.n_cores)], axis=0)


# version 04 (final)
# speedup vs baseline: 1.1195x; 1.0110x over previous
import numpy as np
import ml_dtypes
import concourse.bass as bass
import concourse.mybir as mybir
from concourse.bass_utils import run_bass_kernel_spmd
from contextlib import ExitStack

F32 = mybir.dt.float32
BF16 = mybir.dt.bfloat16
AF = mybir.ActivationFunctionType
ALU = mybir.AluOpType
AX = mybir.AxisListType

MAGIC = float(np.float32(1.5 * 2 ** 23))
NEG_QMIN = -128.0
QMAX = 127.0


class Cfg:
    def __init__(self, T=2048, D=1024, H=16, I=2816, n_cores=8):
        self.T, self.D, self.H, self.I = T, D, H, I
        self.n_cores = n_cores
        self.DH = D // H
        self.NT = T // 128
        self.ND = D // 128
        self.NI = I // 128
        self.eps = 1e-5


class Sem:
    def __init__(self, nc, es, name):
        self.h = es.enter_context(nc.semaphore(name))
        self.n = 0

    def bump(self, k=1):
        self.n += k
        return self.n


class Env:
    ENGINES = ("sync", "tensor", "vector", "scalar", "gpsimd")

    def __init__(self, nc, es):
        self.nc = nc
        self.es = es
        self.ops = {e: [] for e in self.ENGINES}

    def sem(self, name):
        return Sem(self.nc, self.es, name)

    def emit(self, eng, f):
        self.ops[eng].append(f)

    def op(self, eng, meth, *, inc=None, **kw):
        if inc is not None:
            sem, k = inc
            sem.bump(k)
            tgt = (sem.h, k)
        else:
            tgt = None

        def f(e, meth=meth, kw=kw, tgt=tgt):
            ins = getattr(e, meth)(**kw)
            if tgt is not None:
                ins.then_inc(tgt[0], tgt[1])
        self.emit(eng, f)

    def wait(self, eng, sem, val=None):
        v = sem.n if val is None else val
        if v <= 0:
            return
        self.emit(eng, lambda e, h=sem.h, v=v: e.wait_ge(h, v))

    def wait_g(self, eng, guard):
        if guard is None:
            return
        sem, v = guard
        self.wait(eng, sem, v)

    def dma_dbg(self, eng, sem, *, out, in_):
        if sem.n > 0:
            self.wait(eng, sem, sem.n)
        sem.bump(16)
        tgt = sem.h

        def f(e, out=out, in_=in_, tgt=tgt):
            e.dma_start(out=out, in_=in_, allow_slow_non_contiguous=True).then_inc(tgt, 16)
        self.emit(eng, f)

    def dma(self, eng, sem, *, out, in_, throttle=True, accum=False):
        if throttle and sem.n > 0:
            self.wait(eng, sem, sem.n)
        sem.bump(16)
        tgt = sem.h

        def f(e, out=out, in_=in_, tgt=tgt, accum=accum):
            if accum:
                e.dma_start(out=out, in_=in_, accum_op=ALU.add).then_inc(tgt, 16)
            else:
                e.dma_start(out=out, in_=in_).then_inc(tgt, 16)
        self.emit(eng, f)

    def drain_inc(self, eng, sem=None, k=1):
        if sem is not None:
            sem.bump(k)
            self.emit(eng, lambda e, h=sem.h, k=k: e.drain().then_inc(h, k))
        else:
            self.emit(eng, lambda e: e.drain())

    def barrier(self, bar):
        base = bar.n
        for e in self.ENGINES:
            bar.bump(1)
            self.emit(e, lambda en, h=bar.h: en.sem_inc(h, 1))
        for e in self.ENGINES:
            self.wait(e, bar, base + len(self.ENGINES))

    def replay(self):
        with self.nc.Block() as block:
            for eng in self.ENGINES:
                if not self.ops[eng]:
                    continue
                deco = getattr(block, eng)

                def body(e, eng=eng):
                    for f in self.ops[eng]:
                        f(e)
                deco(body)


def build_consts(cfg):
    c = {}
    c["ident_f32"] = np.eye(128, dtype=np.float32)
    c["ident_bf16"] = np.eye(128, dtype=np.float32).astype(ml_dtypes.bfloat16)
    k = np.arange(128)[:, None]
    q = np.arange(128)[None, :]
    c["tri"] = (q >= k).astype(np.float32)
    DH = cfg.DH
    theta = (10000.0 ** (-2.0 * np.arange(DH // 2, dtype=np.float32) / DH)).astype(np.float32)
    pos = np.arange(cfg.T, dtype=np.float32)
    ang = (pos[:, None] * theta[None, :]).astype(np.float32)
    c["cos"] = np.cos(ang).astype(np.float32).reshape(cfg.NT, 128, DH // 2).transpose(1, 0, 2).copy()
    c["sin"] = np.sin(ang).astype(np.float32).reshape(cfg.NT, 128, DH // 2).transpose(1, 0, 2).copy()
    return c


class MK:
    def __init__(self, cfg, with_attn=True):
        self.cfg = cfg
        self.debug_taps = False
        self.with_attn = with_attn
        self.nc = bass.Bass("TRN2", target_bir_lowering=False, detect_race_conditions=False)
        self.es = ExitStack()
        self.env = Env(self.nc, self.es)
        self.ccin_dG = 0
        self._build()

    def sb(self, es, name, shape, dt=F32):
        return es.enter_context(self.nc.sbuf_tensor(name, shape, dt))

    def ps(self, es, name, shape):
        return es.enter_context(self.nc.psum_tensor(name, shape, F32))

    def _build(self):
        cfg, nc, env, es = self.cfg, self.nc, self.env, self.es
        T, D, H, I = cfg.T, cfg.D, cfg.H, cfg.I
        NT, ND, NI, DH = cfg.NT, cfg.ND, cfg.NI, cfg.DH

        self.x_in = nc.dram_tensor("x", [T, D], F32, kind="ExternalInput").ap()
        self.w = {}
        for nm, shp in [("wq", [D, D]), ("wk", [D, D]), ("wv", [D, D]), ("wo", [D, D]),
                        ("wg", [I, D]), ("wu", [I, D]), ("wd", [D, I])]:
            self.w[nm] = nc.dram_tensor(nm, shp, F32, kind="ExternalInput").ap()
        self.clipx = nc.dram_tensor("clipx", [1, 1], F32, kind="ExternalInput").ap()
        self.clipd = nc.dram_tensor("clipd", [1, 1], F32, kind="ExternalInput").ap()
        self.out = nc.dram_tensor("out", [T, D], F32, kind="ExternalOutput").ap()

        self.hpre_scr = nc.dram_tensor("hpre_scr", [I, T], F32, kind="Internal").ap()
        self.x1_scr = nc.dram_tensor("x1_scr", [T, D], F32, kind="Internal").ap()
        self.x2_scr = nc.dram_tensor("x2_scr", [T, D], F32, kind="Internal").ap()
        self.qr_scr = nc.dram_tensor("qr_scr", [D, T], F32, kind="Internal").ap()
        self.cc_in = nc.dram_tensor("cc_in", [1, 16], F32, kind="Internal")
        self.cc_out = nc.dram_tensor("cc_out", [1, 16], F32, kind="Internal")

        cst = build_consts(cfg)
        ident_f_d = nc.inline_tensor(cst["ident_f32"], name="ident_f").ap()
        ident_b_d = nc.inline_tensor(cst["ident_bf16"], name="ident_b").ap()
        tri_d = nc.inline_tensor(cst["tri"], name="tri_c").ap()
        cos_d = nc.inline_tensor(cst["cos"], name="cos_c").ap()
        sin_d = nc.inline_tensor(cst["sin"], name="sin_c").ap()

        s = self.sb
        self.identF = s(es, "identF", [128, 128])
        self.identB = s(es, "identB", [128, 128], BF16)
        self.tri = s(es, "tri", [128, 128])
        self.cC = s(es, "cC", [128, 1])
        self.ones_r = s(es, "ones_r", [1, 128])
        self.cc_sb = s(es, "cc_sb", [1, 16])
        self.gmax_bc = s(es, "gmax_bc", [128, 1])
        self.clipx_bc = s(es, "clipx_bc", [128, 1])
        self.clipd_bc = s(es, "clipd_bc", [128, 1])
        self.sig_t = s(es, "sig_t", [128, 1])
        self.scale_t = s(es, "scale_t", [128, 1])
        self.inv_bc = s(es, "inv_bc", [128, 1])
        self.sc127_bc = s(es, "sc127_bc", [128, 1])
        self.sums = s(es, "sums", [128, NT])
        self.sumsq = s(es, "sumsq", [128, NT])
        self.mean16 = s(es, "mean16", [128, NT])
        self.msq16 = s(es, "msq16", [128, NT])
        self.rstd = s(es, "rstd", [128, NT])
        self.negmr = s(es, "negmr", [128, NT])
        self.stA = s(es, "stA", [128, NT])
        self.stB = s(es, "stB", [128, NT])
        self.permax = s(es, "permax", [128, NT])
        self.xmin16 = s(es, "xmin16", [128, NT])
        self.lmax1 = s(es, "lmax1", [128, 1])
        self.rstd2 = s(es, "rstd2", [128, NT])
        self.negmr2 = s(es, "negmr2", [128, NT])

        e = env
        self.initS = e.sem("initS")
        self.ccD = e.sem("ccD")
        self.tP = e.sem("tP")
        self.vP = e.sem("vP")
        self.sP = e.sem("sP")
        self.ccS = e.sem("ccS")
        self.bar = e.sem("bar")

        self.cos_d, self.sin_d = cos_d, sin_d
        for dst, srct in [(self.identF, ident_f_d), (self.identB, ident_b_d), (self.tri, tri_d)]:
            e.dma("sync", self.initS, out=dst[:], in_=srct)
        e.dma("sync", self.initS, out=self.clipx_bc[:], in_=self.clipx.broadcast_to([128, 1]))
        e.dma("sync", self.initS, out=self.clipd_bc[:], in_=self.clipd.broadcast_to([128, 1]))
        self.eps_bc = s(es, "eps_bc", [128, 1])
        e.op("vector", "memset", ap=self.eps_bc[:], constant=self.cfg.eps)
        nc.const_aps.register(F32, self.cfg.eps, self.eps_bc[:])
        e.op("vector", "memset", ap=self.cC[:], constant=MAGIC)
        e.op("vector", "memset", ap=self.ones_r[:], constant=1.0)
        e.op("vector", "memset", ap=self.cc_sb[:], constant=0.0)
        e.drain_inc("vector", self.vP, 1)
        self.init_v = self.vP.n

        self.emit_ffn("f0", self.x_in,
                      write_out=self.x1_scr if self.with_attn else self.out,
                      ln2_stats=self.with_attn)
        if self.with_attn:
            self.emit_attn()
            self.emit_ffn("f1", self.x2_scr, write_out=self.out, ln2_stats=False)
        env.barrier(self.bar)
        env.replay()
        es.close()

    def _fold_and_allreduce(self, ps_small):
        e, tP, vP, ccD, ccS = self.env, self.tP, self.vP, self.ccD, self.ccS
        e.wait("tensor", vP, vP.n)
        e.wait("tensor", self.initS, self.initS.n)
        e.op("tensor", "transpose", out=ps_small[0:1, :], in_=self.lmax1[:],
             identity=self.identF[:], inc=(tP, 1))
        e.wait("vector", tP, tP.n)
        e.wait("vector", ccD, self.ccin_dG)
        e.wait("vector", self.vP, self.init_v)
        e.op("vector", "tensor_reduce", out=self.cc_sb[0:1, 0:1], in_=ps_small[0:1, :], axis=AX.X,
             op=ALU.max)
        e.drain_inc("vector", vP, 1)
        e.wait("gpsimd", vP, vP.n)
        e.dma("gpsimd", ccD, out=self.cc_in.ap(), in_=self.cc_sb[:])
        self.ccin_dG = ccD.n
        e.wait("gpsimd", ccD, ccD.n)
        e.op("gpsimd", "collective_compute",
             kind="AllReduce", op=ALU.max, replica_groups=[list(range(self.cfg.n_cores))],
             ins=[self.cc_in.ap().opt()], outs=[self.cc_out.ap().opt()], inc=(ccS, 1))
        e.wait("gpsimd", ccS, ccS.n)
        e.dma("gpsimd", ccD, out=self.gmax_bc[:],
              in_=self.cc_out.ap()[0:1, 0:1].broadcast_to([128, 1]))
        return ccD.n

    def _scale_path(self, clip_bc, g_gmax, extra_mul, ps_small):
        e, tP, vP, sP = self.env, self.tP, self.vP, self.sP
        e.wait("scalar", self.initS, self.initS.n)
        e.wait("scalar", vP, vP.n)
        e.op("scalar", "activation", out=self.sig_t[:], in_=clip_bc[:], func=AF.Sigmoid,
             inc=(sP, 1))
        e.wait("vector", sP, sP.n)
        e.wait("vector", self.ccD, g_gmax)
        e.op("vector", "scalar_tensor_tensor", out=self.scale_t[:], in0=self.sig_t[:],
             scalar=1.0 / QMAX, in1=self.gmax_bc[:], op0=ALU.mult, op1=ALU.mult)
        e.drain_inc("vector")
        e.op("vector", "reciprocal", out=self.inv_bc[:], in_=self.scale_t[:])
        e.op("vector", "tensor_scalar", out=self.sc127_bc[:], in0=self.scale_t[:],
             scalar1=float(extra_mul), scalar2=0.0, op0=ALU.mult, op1=ALU.bypass)
        e.drain_inc("vector", vP, 1)
        return vP.n

    def emit_ffn(self, ph, x_src, write_out, ln2_stats):
        cfg, nc, e = self.cfg, self.nc, self.env
        T, D, I, NT, ND, NI = cfg.T, cfg.D, cfg.I, cfg.NT, cfg.ND, cfg.NI
        st = {}
        tP, vP, sP = self.tP, self.vP, self.sP
        xld = e.sem(ph + "xld")
        wS = e.sem(ph + "wS")
        wdS = e.sem(ph + "wdS")
        hrS = e.sem(ph + "hrS")
        hpS = e.sem(ph + "hpS")
        xoS = e.sem(ph + "xoS")
        pes = ExitStack()
        s, p = self.sb, self.ps

        qwdT = s(pes, ph + "qwdT", [128, NI, D], BF16)
        ln_b = [s(pes, ph + f"ln{i}", [128, D]) for i in range(2)]
        qb_b = [s(pes, ph + f"qb{i}", [128, max(D, I)], BF16) for i in range(2)]
        xw_b = [s(pes, ph + f"xw{i}", [128, D]) for i in range(2)]
        sq_scr = s(pes, ph + "sq_scr", [128, D])
        rowvec = s(pes, ph + "rowvec", [1, D])
        ROWSCALE = s(pes, ph + "ROWSCALE", [128, D])
        ps_small = p(pes, ph + "ps_small", [128, 128])
        ps_tp = [es2.enter_context(nc.psum_tensor(ph + f"ps_tp{i}", [128, 128], BF16))
                 for i, es2 in ((0, pes), (1, pes))]
        tp_guard = [None, None]

        xr = x_src.rearrange("(tt p) d -> p tt d", p=128)

        def ln_stat_finish(rstd_t, negmr_t):
            e.drain_inc("vector")
            e.op("vector", "tensor_scalar", out=self.mean16[:], in0=self.sums[:],
                 scalar1=1.0 / D, scalar2=0.0, op0=ALU.mult, op1=ALU.bypass)
            e.op("vector", "tensor_scalar", out=self.sumsq[:], in0=self.sumsq[:],
                 scalar1=1.0 / D, scalar2=0.0, op0=ALU.mult, op1=ALU.bypass)
            e.drain_inc("vector")
            e.op("vector", "tensor_tensor", out=self.msq16[:], in0=self.mean16[:],
                 in1=self.mean16[:], op=ALU.mult)
            e.drain_inc("vector")
            e.op("vector", "tensor_tensor", out=self.sumsq[:], in0=self.sumsq[:],
                 in1=self.msq16[:], op=ALU.subtract)
            e.drain_inc("vector", vP, 1)
            e.wait("scalar", vP, vP.n)
            e.op("scalar", "activation", out=self.stA[:], in_=self.sumsq[:],
                 func=AF.Sqrt, bias=self.cfg.eps, inc=(sP, 1))
            e.wait("vector", sP, sP.n)
            e.op("vector", "reciprocal", out=rstd_t[:], in_=self.stA[:])
            e.drain_inc("vector")
            e.op("vector", "scalar_tensor_tensor", out=negmr_t[:], in0=self.mean16[:],
                 scalar=-1.0, in1=rstd_t[:], op0=ALU.mult, op1=ALU.mult)
            e.drain_inc("vector", vP, 1)
            return vP.n

        for tt in range(NT):
            b = tt % 2
            if tt >= 2:
                e.wait("sync", vP, st["st_red%d" % (tt - 2)])
            e.dma("sync", xld, out=xw_b[b][:], in_=xr[:, tt, :])
            e.wait("scalar", xld, xld.n)
            if tt >= 2:
                e.wait("scalar", vP, st["sq_red%d" % (tt - 2)])
            e.op("scalar", "activation", out=ln_b[b][:], in_=xw_b[b][:], func=AF.Square,
                 inc=(sP, 1))
            st["p1a_sq%d" % tt] = sP.n
            e.wait("vector", xld, xld.n)
            e.wait("vector", sP, sP.n)
            e.op("vector", "reduce_sum", out=self.sumsq[:, tt:tt + 1], in_=ln_b[b][:], axis=AX.X,
                 inc=(vP, 1))
            st["sq_red%d" % tt] = vP.n
            e.op("vector", "reduce_sum", out=self.sums[:, tt:tt + 1], in_=xw_b[b][:], axis=AX.X,
                 inc=(vP, 1))
            e.op("vector", "tensor_reduce", out=self.permax[:, tt:tt + 1], in_=xw_b[b][:],
                 axis=AX.X, op=ALU.max, inc=(vP, 1))
            e.op("vector", "tensor_reduce", out=self.xmin16[:, tt:tt + 1], in_=xw_b[b][:],
                 axis=AX.X, op=ALU.min, inc=(vP, 1))
            st["st_red%d" % tt] = vP.n
        v_st1 = ln_stat_finish(self.rstd, self.negmr)

        e.drain_inc("vector")
        e.op("vector", "tensor_tensor", out=self.permax[:], in0=self.permax[:],
             in1=self.mean16[:], op=ALU.subtract)
        e.op("vector", "tensor_tensor", out=self.xmin16[:], in0=self.mean16[:],
             in1=self.xmin16[:], op=ALU.subtract)
        e.drain_inc("vector")
        e.op("vector", "tensor_tensor", out=self.permax[:], in0=self.permax[:],
             in1=self.xmin16[:], op=ALU.max)
        e.drain_inc("vector")
        e.op("vector", "tensor_tensor", out=self.permax[:], in0=self.permax[:],
             in1=self.rstd[:], op=ALU.mult)
        e.drain_inc("vector")
        e.op("vector", "tensor_reduce", out=self.lmax1[:], in_=self.permax[:], axis=AX.X, op=ALU.max)
        e.drain_inc("vector", vP, 1)
        g_gmax = self._fold_and_allreduce(ps_small)
        v_sc = self._scale_path(self.clipx_bc, g_gmax, 1.0 / QMAX, ps_small)
        e.op("vector", "tensor_scalar", out=self.stA[:], in0=self.rstd[:], scalar1=self.inv_bc[:, 0:1],
             scalar2=0.0, op0=ALU.mult, op1=ALU.bypass)
        e.op("vector", "tensor_scalar", out=self.stB[:], in0=self.negmr[:], scalar1=self.inv_bc[:, 0:1],
             scalar2=0.0, op0=ALU.mult, op1=ALU.bypass)
        e.drain_inc("vector", vP, 1)
        v_ab = vP.n

        sub1 = ExitStack()
        qxT = s(sub1, ph + "qxT", [128, ND, T], BF16)
        e.wait("scalar", vP, v_ab)
        for tt in range(NT):
            b = tt % 2
            if tt >= 2:
                e.wait("sync", sP, st["qx_act%d" % (tt - 2)])
            else:
                e.wait("sync", vP, st["st_red%d" % (NT - 1)])
                e.wait("sync", sP, st["p1a_sq%d" % (NT - 1)])
            e.dma("sync", xld, out=xw_b[b][:], in_=xr[:, tt, :])
            e.wait("scalar", xld, xld.n)
            if tt >= 2:
                e.wait("scalar", vP, st["qa_done%d" % (tt - 2)])
                e.wait("scalar", tP, st["qx_tr%d" % (tt - 2)])
            e.op("scalar", "activation", out=ln_b[b][:], in_=xw_b[b][:], func=AF.Identity,
                 scale=self.stA[:, tt:tt + 1], bias=self.stB[:, tt:tt + 1], inc=(sP, 1))
            st["qx_act%d" % tt] = sP.n
            e.wait("vector", sP, sP.n)
            e.op("vector", "tensor_scalar", out=ln_b[b][:], in0=ln_b[b][:], scalar1=MAGIC,
                 scalar2=MAGIC, op0=ALU.add, op1=ALU.subtract)
            e.op("vector", "tensor_scalar", out=qb_b[b][:, 0:D], in0=ln_b[b][:], scalar1=NEG_QMIN,
                 scalar2=QMAX, op0=ALU.max, op1=ALU.min, inc=(vP, 1))
            st["qa_done%d" % tt] = vP.n
            e.wait("tensor", vP, vP.n)
            for dc in range(ND):
                pb = dc % 2
                e.wait_g("tensor", tp_guard[pb])
                e.op("tensor", "transpose", out=ps_tp[pb][:], in_=qb_b[b][:, dc * 128:(dc + 1) * 128],
                     identity=self.identB[:], inc=(tP, 1))
                st["qx_tr%d" % tt] = tP.n
                e.wait("scalar", tP, tP.n)
                e.op("scalar", "copy", out=qxT[:, dc, tt * 128:(tt + 1) * 128], in_=ps_tp[pb][:],
                     inc=(sP, 1))
                tp_guard[pb] = (sP, sP.n)
        s_qx = sP.n
        wg_b = [s(sub1, ph + f"wg{i}", [128, D]) for i in range(2)]
        wu_b = [s(sub1, ph + f"wu{i}", [128, D]) for i in range(2)]
        qwg_b = [s(sub1, ph + f"qwg{i}", [128, ND, 128], BF16) for i in range(2)]
        qwu_b = [s(sub1, ph + f"qwu{i}", [128, ND, 128], BF16) for i in range(2)]
        wt1 = s(sub1, ph + "wt1", [128, max(D, I)])
        coefg = s(sub1, ph + "coefg", [128, 2])
        coefu = s(sub1, ph + "coefu", [128, 2])
        rmax_w = s(sub1, ph + "rmax_w", [128, 2])
        wrec = s(sub1, ph + "wrec", [128, 1])
        silu_b = [s(sub1, ph + f"silu{i}", [128, 512]) for i in range(2)]
        hpre_b = [s(sub1, ph + f"hpre{i}", [128, 1024]) for i in range(2)]
        NHM = NI * (T // 512)
        hmax_arr = s(sub1, ph + "hmax_arr", [128, NHM])
        ps_g = [p(sub1, ph + f"psg{i}", [128, 512]) for i in range(2)]
        ps_u = [p(sub1, ph + f"psu{i}", [128, 512]) for i in range(2)]

        def quant_weight(wraw, wq_out, qwT, coef, par, have_coef, nd):
            e.op("vector", "tensor_reduce", out=rmax_w[:, par:par + 1], in_=wraw[:], axis=AX.X,
                 op=ALU.max, apply_absolute_value=True)
            e.drain_inc("vector")
            e.op("vector", "reciprocal", out=wrec[:], in_=rmax_w[:, par:par + 1])
            e.drain_inc("vector")
            e.op("vector", "tensor_scalar", out=wrec[:], in0=wrec[:], scalar1=QMAX, scalar2=0.0,
                 op0=ALU.mult, op1=ALU.bypass)
            if have_coef:
                e.op("vector", "tensor_tensor", out=coef[:, par:par + 1], in0=rmax_w[:, par:par + 1],
                     in1=self.sc127_bc[:, 0:1], op=ALU.mult)
            e.drain_inc("vector", vP, 1)
            e.wait("scalar", vP, vP.n)
            e.wait("scalar", tP, tP.n)
            e.op("scalar", "activation", out=wq_out[:, 0:nd * 128], in_=wraw[:], func=AF.Identity,
                 scale=wrec[:, 0:1], bias=self.cC[:, 0:1], inc=(sP, 1))
            e.wait("vector", sP, sP.n)
            e.op("vector", "tensor_scalar", out=wq_out[:, 0:nd * 128], in0=wq_out[:, 0:nd * 128],
                 scalar1=MAGIC, scalar2=NEG_QMIN, op0=ALU.subtract, op1=ALU.max)
            e.op("vector", "tensor_scalar", out=qb_b[0][:, 0:nd * 128], in0=wq_out[:, 0:nd * 128],
                 scalar1=QMAX, scalar2=0.0, op0=ALU.min, op1=ALU.bypass, inc=(vP, 1))
            e.wait("tensor", vP, vP.n)
            for dc in range(nd):
                pb = dc % 2
                e.wait_g("tensor", tp_guard[pb])
                e.op("tensor", "transpose", out=ps_tp[pb][:],
                     in_=qb_b[0][:, dc * 128:(dc + 1) * 128],
                     identity=self.identB[:], inc=(tP, 1))
                e.wait("scalar", tP, tP.n)
                e.op("scalar", "copy", out=qwT[:, dc, :],
                     in_=ps_tp[pb][:], inc=(sP, 1))
                tp_guard[pb] = (sP, sP.n)
            return sP.n

        NTC = T // 512
        for it in range(NI):
            b = it % 2
            if it >= 2:
                e.wait("sync", sP, st["wq_done%d" % (it - 2)])
            e.dma("sync", wS, out=wg_b[b][:], in_=self.w["wg"][it * 128:(it + 1) * 128, :])
            e.dma("sync", wS, out=wu_b[b][:], in_=self.w["wu"][it * 128:(it + 1) * 128, :])
            ld_w = wS.n
            e.wait("vector", wS, ld_w)
            e.wait("vector", vP, v_sc)
            if it >= 2:
                e.wait("scalar", tP, st["mm_done%d" % (it - 2)])
                e.wait("vector", sP, st["silu_done%d" % (it - 2)])
            quant_weight(wg_b[b], wt1, qwg_b[b], coefg, b, True, ND)
            s_wq = quant_weight(wu_b[b], wt1, qwu_b[b], coefu, b, True, ND)
            st["wq_done%d" % it] = s_wq
            e.wait("tensor", sP, s_wq)
            if it == 0:
                e.wait("tensor", sP, s_qx)
            for tc in range(NTC):
                pb = tc % 2
                k = it * NTC + tc
                if k >= 2:
                    e.wait("tensor", vP, st["gu_ev%d" % (k - 2)])
                for dc in range(ND):
                    e.op("tensor", "matmul", out=ps_g[pb][:], lhsT=qwg_b[b][:, dc, :],
                         rhs=qxT[:, dc, tc * 512:(tc + 1) * 512],
                         start=(dc == 0), stop=(dc == ND - 1))
                for dc in range(ND):
                    e.op("tensor", "matmul", out=ps_u[pb][:], lhsT=qwu_b[b][:, dc, :],
                         rhs=qxT[:, dc, tc * 512:(tc + 1) * 512],
                         start=(dc == 0), stop=(dc == ND - 1),
                         inc=(tP, 1) if dc == ND - 1 else None)
                st["mm_done%d" % it] = tP.n
                e.wait("scalar", tP, tP.n)
                e.op("scalar", "activation", out=silu_b[pb][:], in_=ps_g[pb][:], func=AF.Silu,
                     scale=coefg[:, b:b + 1], inc=(sP, 1))
                st["silu_done%d" % it] = sP.n
                hb = (tc // 2) % 2
                half = tc % 2
                e.wait("vector", sP, sP.n)
                if "hpre_dma%d" % hb in st:
                    e.wait("vector", hpS, st["hpre_dma%d" % hb])
                e.op("vector", "scalar_tensor_tensor",
                     out=hpre_b[hb][:, half * 512:half * 512 + 512],
                     in0=ps_u[pb][:], scalar=coefu[:, b:b + 1], in1=silu_b[pb][:],
                     op0=ALU.mult, op1=ALU.mult)
                e.op("vector", "tensor_reduce", out=hmax_arr[:, k:k + 1],
                     in_=hpre_b[hb][:, half * 512:half * 512 + 512], axis=AX.X,
                     op=ALU.max, apply_absolute_value=True, inc=(vP, 1))
                st["gu_ev%d" % k] = vP.n
                if half == 1:
                    e.wait("gpsimd", vP, vP.n)
                    e.dma("gpsimd", hpS,
                          out=self.hpre_scr.rearrange("(ic p) t -> p ic t", p=128)[
                              :, it, (tc // 2) * 1024:(tc // 2) * 1024 + 1024],
                          in_=hpre_b[hb][:])
                    st["hpre_dma%d" % hb] = hpS.n
        v_gu = vP.n
        g_hpre = hpS.n

        e.wait("vector", vP, v_gu)
        e.drain_inc("vector")
        e.op("vector", "tensor_reduce", out=self.lmax1[:], in_=hmax_arr[:], axis=AX.X, op=ALU.max)
        e.drain_inc("vector", vP, 1)
        g_gmax2 = self._fold_and_allreduce(ps_small)
        e.wait("tensor", vP, vP.n)
        wd_b = [s(sub1, ph + "wd0", [128, I])]
        NHC = D // 128
        for hc in range(NHC):
            b = 0
            if hc >= 1:
                e.wait("sync", sP, st["wd_done%d" % (hc - 1)])
            e.dma("sync", wdS, out=wd_b[b][:], in_=self.w["wd"][hc * 128:(hc + 1) * 128, :])
            e.wait("vector", wdS, wdS.n)
            if hc >= 1:
                e.wait("vector", tP, st["rv_tr%d" % (hc - 1)])
            e.op("vector", "tensor_reduce", out=rmax_w[:, 0:1], in_=wd_b[b][:], axis=AX.X,
                 op=ALU.max, apply_absolute_value=True)
            e.drain_inc("vector")
            e.op("vector", "reciprocal", out=wrec[:], in_=rmax_w[:, 0:1])
            e.drain_inc("vector")
            e.op("vector", "tensor_scalar", out=wrec[:], in0=wrec[:], scalar1=QMAX, scalar2=0.0,
                 op0=ALU.mult, op1=ALU.bypass)
            e.drain_inc("vector", vP, 1)
            v_rm = vP.n
            e.wait("tensor", vP, v_rm)
            e.op("tensor", "transpose", out=ps_small[0:1, :], in_=rmax_w[:, 0:1],
                 identity=self.identF[:], inc=(tP, 1))
            st["rv_tr%d" % hc] = tP.n
            e.wait("vector", tP, tP.n)
            e.op("vector", "tensor_copy", out=rowvec[0:1, hc * 128:(hc + 1) * 128],
                 in_=ps_small[0:1, :], inc=(vP, 1))
            e.wait("scalar", vP, v_rm)
            e.wait("scalar", tP, tP.n)
            e.op("scalar", "activation", out=wt1[:, 0:I], in_=wd_b[b][:], func=AF.Identity,
                 scale=wrec[:, 0:1], bias=self.cC[:, 0:1], inc=(sP, 1))
            e.wait("vector", sP, sP.n)
            e.op("vector", "tensor_scalar", out=wt1[:, 0:I], in0=wt1[:, 0:I], scalar1=MAGIC,
                 scalar2=NEG_QMIN, op0=ALU.subtract, op1=ALU.max)
            e.op("vector", "tensor_scalar", out=qb_b[1][:, 0:I], in0=wt1[:, 0:I], scalar1=QMAX,
                 scalar2=0.0, op0=ALU.min, op1=ALU.bypass, inc=(vP, 1))
            e.wait("tensor", vP, vP.n)
            for ic in range(NI):
                pb = ic % 2
                e.wait_g("tensor", tp_guard[pb])
                e.op("tensor", "transpose", out=ps_tp[pb][:], in_=qb_b[1][:, ic * 128:(ic + 1) * 128],
                     identity=self.identB[:], inc=(tP, 1))
                e.wait("scalar", tP, tP.n)
                e.op("scalar", "copy", out=qwdT[:, ic, hc * 128:(hc + 1) * 128], in_=ps_tp[pb][:],
                     inc=(sP, 1))
                tp_guard[pb] = (sP, sP.n)
            st["wd_done%d" % hc] = sP.n
        s_wd = sP.n
        v_sch = self._scale_path(self.clipd_bc, g_gmax2, 0.5 / QMAX, ps_small)


        e.wait("vector", sP, s_wd)
        e.wait("vector", vP, v_sch)
        e.drain_inc("vector")
        e.op("vector", "tensor_scalar", out=rowvec[:], in0=rowvec[:],
             scalar1=self.sc127_bc[0:1, 0:1], scalar2=0.0, op0=ALU.mult, op1=ALU.bypass)
        e.drain_inc("vector", vP, 1)
        v_rv = vP.n
        sub1.close()
        sub2 = ExitStack()
        NDC = D // 512
        NIH = (NI + 1) // 2
        ps_d = [p(sub2, ph + f"psd{i}", [128, 512]) for i in range(2 * NDC)]
        hraw_b = [s(sub2, ph + f"hraw{i}", [128, NIH, 512]) for i in range(2)]
        qh_b = [s(sub2, ph + f"qh{i}", [128, NI, 512], BF16) for i in range(2)]
        hq1 = s(sub2, ph + "hq1", [128, 512])
        hq1b = s(sub2, ph + "hq1b", [128, 512])
        hqs = [hq1, hq1b]
        evt = s(sub2, ph + "evt", [128, 512])
        xo_b = [s(sub2, ph + f"xo{i}", [128, D]) for i in range(2)]

        e.wait("tensor", vP, v_rv)
        for c in range(NDC):
            e.op("tensor", "matmul", out=ps_d[c][:], lhsT=self.ones_r[:],
                 rhs=rowvec[0:1, c * 512:(c + 1) * 512], start=True, stop=True,
                 inc=(tP, 1) if c == NDC - 1 else None)
        e.wait("vector", tP, tP.n)
        for c in range(NDC):
            e.op("vector", "tensor_copy", out=ROWSCALE[:, c * 512:(c + 1) * 512],
                 in_=ps_d[c][:], inc=(vP, 1) if c == NDC - 1 else None)
        v_rs = vP.n

        hsr = self.hpre_scr.rearrange("(ic p) t -> p ic t", p=128)
        NTC2 = T // 512
        v_invh = v_sch
        hrG = e.sem(ph + "hrG")
        def issue_hraw(tc):
            for j, eng, sem in ((0, "sync", hrS), (1, "gpsimd", hrG)):
                lo = j * NIH
                hi = min(NI, lo + NIH)
                if tc >= 1:
                    e.wait(eng, sP, st["hq_h%d_%d" % (tc - 1, j)])
                elif j == 0:
                    e.wait(eng, hpS, g_hpre)
                e.dma(eng, sem, out=hraw_b[j][:, 0:hi - lo, :],
                      in_=hsr[:, lo:hi, tc * 512:(tc + 1) * 512])
                st["hld%d_%d" % (tc, j)] = sem.n

        issue_hraw(0)
        for tc in range(NTC2):
            b = tc % 2
            e.wait("scalar", vP, v_invh)
            e.wait("scalar", vP, vP.n)
            if tc >= 2:
                e.wait("scalar", tP, st["dmm%d" % (tc - 2)])
            for ic in range(NI):
                j = ic // NIH
                hq = hqs[ic % 2]
                if ic % NIH == 0:
                    e.wait("scalar", hrS if j == 0 else hrG, st["hld%d_%d" % (tc, j)])
                if ic >= 2:
                    e.wait("scalar", vP, st["hqv%d" % (ic - 2)])
                e.op("scalar", "activation", out=hq[:], in_=hraw_b[j][:, ic - j * NIH, :],
                     func=AF.Identity, scale=self.inv_bc[:, 0:1], bias=self.cC[:, 0:1],
                     inc=(sP, 1))
                if ic == min(NI, NIH) - 1:
                    st["hq_h%d_0" % tc] = sP.n
                e.wait("vector", sP, sP.n)
                e.op("vector", "tensor_scalar", out=hq[:], in0=hq[:], scalar1=MAGIC,
                     scalar2=NEG_QMIN, op0=ALU.subtract, op1=ALU.max)
                e.op("vector", "tensor_scalar", out=qh_b[b][:, ic, :], in0=hq[:], scalar1=QMAX,
                     scalar2=0.0, op0=ALU.min, op1=ALU.bypass, inc=(vP, 1))
                st["hqv%d" % ic] = vP.n
            st["hq_h%d_1" % tc] = sP.n
            def issue_xld(tl):
                tt = tc * 4 + tl
                if tt >= 2:
                    e.wait("sync", vP, st["d_ev%d" % (tt - 2)])
                else:
                    e.wait("sync", sP, st["qx_act%d" % (NT - 1)])
                e.dma("sync", xld, out=xw_b[tl % 2][:], in_=xr[:, tt, :])
                st["xld%d" % tt] = xld.n
            issue_xld(0)
            issue_xld(1)
            if tc + 1 < NTC2:
                issue_hraw(tc + 1)
            v_qh = vP.n
            e.wait("tensor", vP, v_qh)
            if tc == 0:
                e.wait("tensor", sP, s_wd)
            for tl in range(4):
                tt = tc * 4 + tl
                bo = tl % 2
                pd = [ps_d[(tl % 2) * NDC + c] for c in range(NDC)]
                if tt >= 2:
                    e.wait("tensor", vP, st["d_ev%d" % (tt - 2)])
                for ic in range(NI):
                    for c in range(NDC):
                        e.op("tensor", "matmul", out=pd[c][:],
                             lhsT=qh_b[b][:, ic, tl * 128:(tl + 1) * 128],
                             rhs=qwdT[:, ic, c * 512:(c + 1) * 512],
                             start=(ic == 0), stop=(ic == NI - 1),
                             inc=(tP, 1) if (ic == NI - 1 and c == NDC - 1) else None)
                st["dmm%d" % tc] = tP.n
                if tl >= 2:
                    issue_xld(tl)
                ld_x = st["xld%d" % tt]
                e.wait("vector", tP, tP.n)
                e.wait("vector", vP, v_rs)
                e.wait("vector", xld, ld_x)
                e.wait("vector", sP, sP.n)
                if tt >= 2:
                    e.wait("vector", xoS, st["xo_dma%d" % (tt - 2)])
                for c in range(NDC):
                    e.op("vector", "tensor_tensor", out=evt[:], in0=pd[c][:],
                         in1=ROWSCALE[:, c * 512:(c + 1) * 512], op=ALU.mult)
                    e.op("vector", "tensor_tensor", out=xo_b[bo][:, c * 512:(c + 1) * 512],
                         in0=xw_b[bo][:, c * 512:(c + 1) * 512], in1=evt[:],
                         op=ALU.add, inc=(vP, 1) if c == NDC - 1 else None)
                st["d_ev%d" % tt] = vP.n
                if ln2_stats:
                    e.wait("scalar", vP, vP.n)
                    if tt >= 2:
                        e.wait("scalar", vP, st["sq2_red%d" % (tt - 2)])
                    e.op("scalar", "activation", out=hq1[:], in_=xo_b[bo][:, 0:512],
                         func=AF.Square, inc=(sP, 1))
                    e.op("scalar", "activation", out=evt[:], in_=xo_b[bo][:, 512:D],
                         func=AF.Square, inc=(sP, 1)) if D > 512 else None
                    e.wait("vector", sP, sP.n)
                    e.op("vector", "reduce_sum", out=self.sumsq[:, tt:tt + 1], in_=hq1[:],
                         axis=AX.X)
                    if D > 512:
                        e.op("vector", "reduce_sum", out=self.msq16[:, 0:1], in_=evt[:], axis=AX.X)
                        e.drain_inc("vector")
                        e.op("vector", "tensor_tensor", out=self.sumsq[:, tt:tt + 1],
                             in0=self.sumsq[:, tt:tt + 1], in1=self.msq16[:, 0:1], op=ALU.add)
                    e.drain_inc("vector", vP, 1)
                    st["sq2_red%d" % tt] = vP.n
                    e.op("vector", "reduce_sum", out=self.sums[:, tt:tt + 1], in_=xo_b[bo][:],
                         axis=AX.X, inc=(vP, 1))
                e.wait("gpsimd", vP, vP.n)
                e.dma("gpsimd", xoS,
                      out=write_out.rearrange("(tt p) d -> p tt d", p=128)[:, tt, :],
                      in_=xo_b[bo][:])
                st["xo_dma%d" % tt] = xoS.n
        if ln2_stats:
            ln_stat_finish(self.rstd2, self.negmr2)
        e.wait("gpsimd", xoS, xoS.n)
        self.env.barrier(self.bar)
        sub2.close()
        pes.close()

    def emit_attn(self):
        cfg, nc, e = self.cfg, self.nc, self.env
        T, D, H, DH, NT, ND = cfg.T, cfg.D, cfg.H, cfg.DH, cfg.NT, cfg.ND
        NH2 = DH // 2
        NDC = D // 512
        HPC = 512 // DH
        st = {}
        tP, vP, sP = self.tP, self.vP, self.sP
        e2 = self.env
        axS = e2.sem("axS")
        awS = e2.sem("awS")
        qrS = e2.sem("qrS")
        qcS = e2.sem("qcS")
        x2S = e2.sem("x2S")
        pes = ExitStack()
        s, p = self.sb, self.ps

        KrT = s(pes, "KrT", [128, ND, T])
        VS = 66
        Varena = s(pes, "Varena", [128, NT, H * VS])
        WT = s(pes, "WT", [128, ND, D])
        sub_pj = ExitStack()
        cosT = s(sub_pj, "cosT", [128, NT, DH // 2])
        sinT = s(sub_pj, "sinT", [128, NT, DH // 2])
        csS = e2.sem("csS")
        e.dma("sync", csS, out=cosT[:], in_=self.cos_d.rearrange("p t i -> p (t i)"))
        e.dma("sync", csS, out=sinT[:], in_=self.sin_d.rearrange("p t i -> p (t i)"))
        cs_ld = csS.n
        lnT = s(sub_pj, "lnT", [128, ND, 128])
        x1_b = [s(sub_pj, f"ax1_{i}", [128, D]) for i in range(2)]
        ln2_b = [s(sub_pj, f"aln_{i}", [128, D]) for i in range(2)]
        pr_b = ln2_b
        rp_t = s(sub_pj, "rp_t", [128, 512])
        w_b = [s(sub_pj, f"aw_{i}", [128, D]) for i in range(2)]
        ps_pj = [p(sub_pj, "ps_pj%d" % i, [128, 512]) for i in range(2)]
        ps_tp = [p(sub_pj, "ps_tp%d" % i, [128, 128]) for i in range(2)]
        tp_guard = [None, None]
        x2pS = e2.sem("x2pS")
        e.dma("gpsimd", x2pS, out=self.x2_scr, in_=self.x1_scr)

        e.op("vector", "memset",
             ap=Varena[:].rearrange("p t (h s) -> p t h s", h=H)[:, :, :, 64:65],
             constant=1.0, inc=(vP, 1))
        def prep_WT(w_ap):
            e.wait("scalar", tP, tP.n)
            for oc in range(ND):
                b = oc % 2
                if oc >= 2:
                    e.wait("sync", tP, st["aWtr%d" % (oc - 2)])
                e.dma("sync", awS, out=w_b[b][:], in_=w_ap[oc * 128:(oc + 1) * 128, :])
                e.wait("tensor", awS, awS.n)
                for dcc in range(ND):
                    pb = dcc % 2
                    e.wait_g("tensor", tp_guard[pb])
                    e.op("tensor", "transpose", out=ps_tp[pb][:],
                         in_=w_b[b][:, dcc * 128:(dcc + 1) * 128], identity=self.identF[:],
                         inc=(tP, 1))
                    st["aWtr%d" % oc] = tP.n
                    e.wait("scalar", tP, tP.n)
                    e.op("scalar", "copy", out=WT[:, dcc, oc * 128:(oc + 1) * 128], in_=ps_tp[pb][:],
                         inc=(sP, 1))
                    tp_guard[pb] = (sP, sP.n)
            return sP.n

        def pass_proj(w_ap, kind):
            s_w = prep_WT(w_ap)
            for tt in range(NT):
                b = tt % 2
                if tt >= 2:
                    e.wait("sync", sP, st["aln_done%d" % (tt - 2)])
                e.dma("sync", axS, out=x1_b[b][:],
                      in_=self.x1_scr.rearrange("(tt p) d -> p tt d", p=128)[:, tt, :])
                e.wait("scalar", axS, axS.n)
                e.wait("scalar", tP, tP.n)
                e.op("scalar", "activation", out=ln2_b[b][:], in_=x1_b[b][:], func=AF.Identity,
                     scale=self.rstd2[:, tt:tt + 1], bias=self.negmr2[:, tt:tt + 1], inc=(sP, 1))
                st["aln_done%d" % tt] = sP.n
                e.wait("tensor", sP, sP.n)
                for dc in range(ND):
                    pb = dc % 2
                    e.wait_g("tensor", tp_guard[pb])
                    e.op("tensor", "transpose", out=ps_tp[pb][:],
                         in_=ln2_b[b][:, dc * 128:(dc + 1) * 128], identity=self.identF[:],
                         inc=(tP, 1))
                    e.wait("vector", tP, tP.n)
                    if kind == "Q" and "aQdma%d" % (tt - 1) in st:
                        e.wait("vector", qrS, st["aQdma%d" % (tt - 1)])
                    e.op("vector", "tensor_copy", out=lnT[:, dc, :], in_=ps_tp[pb][:],
                         inc=(vP, 1))
                    tp_guard[pb] = (vP, vP.n)
                v_lnT = vP.n
                e.wait("tensor", vP, v_lnT)
                if tt == 0:
                    e.wait("tensor", sP, s_w)
                for c in range(NDC):
                    for dc in range(ND):
                        e.op("tensor", "matmul", out=ps_pj[c][:], lhsT=lnT[:, dc, :],
                             rhs=WT[:, dc, c * 512:(c + 1) * 512],
                             start=(dc == 0), stop=(dc == ND - 1),
                             inc=(tP, 1) if dc == ND - 1 else None)
                mm_n = tP.n
                if kind == "V":
                    e.wait("vector", tP, mm_n)
                    for h in range(H):
                        c = (h * 64) // 512
                        off = (h * 64) % 512
                        e.op("vector", "tensor_copy",
                             out=Varena[:, tt, h * VS:h * VS + 64],
                             in_=ps_pj[c][:, off:off + 64],
                             inc=(vP, 1) if h == H - 1 else None)
                else:
                    e.wait("vector", tP, mm_n)
                    if tt >= 2:
                        e.wait("vector", tP, st["aRtr%d" % (tt - 2)])
                    e.wait("vector", csS, cs_ld)
                    cosb = cosT[:, tt, :].unsqueeze(1).broadcast_to([128, HPC, NH2])
                    sinb = sinT[:, tt, :].unsqueeze(1).broadcast_to([128, HPC, NH2])
                    for c in range(NDC):
                        pj = ps_pj[c][:].rearrange("p (h i two) -> p h i two", h=HPC, two=2)
                        pr = pr_b[b][:, c * 512:(c + 1) * 512].rearrange(
                            "p (h i two) -> p h i two", h=HPC, two=2)
                        rp = rp_t[:, 0:HPC * NH2].rearrange("p (h i) -> p h i", h=HPC)
                        e.op("vector", "tensor_tensor", out=pr[:, :, :, 0], in0=pj[:, :, :, 0],
                             in1=cosb, op=ALU.mult)
                        e.op("vector", "tensor_tensor", out=rp, in0=pj[:, :, :, 1], in1=sinb,
                             op=ALU.mult)
                        e.op("vector", "tensor_tensor", out=pr[:, :, :, 0], in0=pr[:, :, :, 0],
                             in1=rp, op=ALU.subtract)
                        e.op("vector", "tensor_tensor", out=pr[:, :, :, 1], in0=pj[:, :, :, 0],
                             in1=sinb, op=ALU.mult)
                        e.op("vector", "tensor_tensor", out=rp, in0=pj[:, :, :, 1], in1=cosb,
                             op=ALU.mult)
                        e.op("vector", "tensor_tensor", out=pr[:, :, :, 1], in0=pr[:, :, :, 1],
                             in1=rp, op=ALU.add, inc=(vP, 1))
                    v_rope = vP.n
                    e.wait("tensor", vP, v_rope)
                    for dc in range(ND):
                        pb = dc % 2
                        e.wait_g("tensor", tp_guard[pb])
                        e.op("tensor", "transpose", out=ps_tp[pb][:],
                             in_=pr_b[b][:, dc * 128:(dc + 1) * 128], identity=self.identF[:],
                             inc=(tP, 1))
                        st["aRtr%d" % tt] = tP.n
                        e.wait("scalar", tP, tP.n)
                        if kind == "K":
                            e.op("scalar", "copy", out=KrT[:, dc, tt * 128:(tt + 1) * 128],
                                 in_=ps_tp[pb][:], inc=(sP, 1))
                        else:
                            if dc == 0 and "aQdma%d" % (tt - 1) in st:
                                e.wait("scalar", qrS, st["aQdma%d" % (tt - 1)])
                            e.op("scalar", "copy", out=lnT[:, dc, :], in_=ps_tp[pb][:], inc=(sP, 1))
                        tp_guard[pb] = (sP, sP.n)
                    if kind == "Q":
                        e.wait("gpsimd", sP, sP.n)
                        e.dma("gpsimd", qrS,
                              out=self.qr_scr.rearrange("(dc p) t -> p dc t", p=128)[
                                  :, :, tt * 128:(tt + 1) * 128],
                              in_=lnT[:])
                        st["aQdma%d" % tt] = qrS.n

        pass_proj(self.w["wk"], "K")
        s_kr = sP.n
        pass_proj(self.w["wv"], "V")
        v_kv = vP.n
        pass_proj(self.w["wq"], "Q")
        g_qr = qrS.n
        s_wo = prep_WT(self.w["wo"])
        sub_pj.close()

        QrC = s(pes, "QrC", [128, ND, 512])
        E_b = [s(pes, f"aE{i}", [128, 512]) for i in range(2)]
        Dbc = s(pes, "Dbc", [64, 512])
        dinv = Dbc[0:1, :]
        headsN = s(pes, "headsN", [128, ND, 512])
        x2_b = [s(pes, "ax2s", [128, D])] * 2
        ps_s = [p(pes, "ps_s%d" % i, [128, 512]) for i in range(2)]
        ps_av = [p(pes, "ps_av%d" % i, [128, 512]) for i in range(2)]
        ps_o = [p(pes, "ps_o%d" % i, [128, 512]) for i in range(2)]

        g_ps_s = [None, None]
        g_ps_av = [None, None]
        g_ps_o = [None, None]
        g_E = [None, None]
        g_x2 = [None, None]

        NQC = T // 512
        e.wait("tensor", sP, s_wo)
        e.wait("tensor", vP, v_kv)
        for qc in range(NQC):
            if qc >= 1:
                e.wait("sync", tP, st["aQcons%d" % (qc - 1)])
            e.wait("sync", qrS, g_qr)
            e.dma("sync", qcS, out=QrC[:],
                  in_=self.qr_scr.rearrange("(dc p) t -> p dc t", p=128)[
                      :, :, qc * 512:(qc + 1) * 512])
            ld_q = qcS.n
            e.wait("tensor", qcS, ld_q)
            nkt = qc * 4 + 4
            for h in range(H):
                dc = h // 2
                pb2 = (h % 2) * 64
                avp = ps_av[h % 2]
                e.wait_g("tensor", g_ps_av[h % 2])

                def emit_S(kt):
                    j = kt - qc * 4
                    q0 = max(0, j * 128)
                    qn = 512 - q0
                    slot = kt % 2
                    e.wait_g("tensor", g_ps_s[slot])
                    e.op("tensor", "matmul", out=ps_s[slot][:, 0:qn],
                         lhsT=KrT[pb2:pb2 + 64, dc, kt * 128:(kt + 1) * 128],
                         rhs=QrC[pb2:pb2 + 64, dc, q0:512],
                         start=True, stop=True, inc=(tP, 1))
                    st["aSmm%d" % slot] = tP.n

                def emit_exp(kt):
                    j = kt - qc * 4
                    q0 = max(0, j * 128)
                    qn = 512 - q0
                    slot = kt % 2
                    e.wait("scalar", tP, st["aSmm%d" % slot])
                    e.wait_g("scalar", g_E[slot])
                    e.op("scalar", "activation", out=E_b[slot][:, 0:qn], in_=ps_s[slot][:, 0:qn],
                         func=AF.Exp, scale=0.125, inc=(sP, 1))
                    g_ps_s[slot] = (sP, sP.n)
                    if j >= 0:
                        e.wait("vector", sP, sP.n)
                        e.op("vector", "tensor_tensor", out=E_b[slot][:, 0:128],
                             in0=E_b[slot][:, 0:128], in1=self.tri[:], op=ALU.mult,
                             inc=(vP, 1))
                        return (vP, vP.n)
                    return (sP, sP.n)

                def emit_AV(kt, e_ready):
                    j = kt - qc * 4
                    q0 = max(0, j * 128)
                    qn = 512 - q0
                    slot = kt % 2
                    e.wait_g("tensor", e_ready)
                    e.op("tensor", "matmul", out=avp[0:65, q0:512],
                         lhsT=Varena[:, kt, h * VS:h * VS + 65],
                         rhs=E_b[slot][:, 0:qn],
                         start=(kt == 0), stop=(kt == nkt - 1), inc=(tP, 1))
                    g_E[slot] = (tP, tP.n)

                emit_S(0)
                er = emit_exp(0)
                for kt in range(nkt - 1):
                    emit_S(kt + 1)
                    er_next_ready = er
                    emit_AV(kt, er_next_ready)
                    er = emit_exp(kt + 1)
                emit_AV(nkt - 1, er)
                e.wait("vector", tP, tP.n)
                e.op("vector", "tensor_copy", out=dinv, in_=avp[64:65, :])
                e.drain_inc("vector")
                e.op("vector", "reciprocal", out=dinv, in_=dinv)
                e.drain_inc("vector", vP, 1)
                e.wait_g("tensor", g_ps_s[0])
                e.wait("tensor", vP, vP.n)
                e.op("tensor", "matmul", out=ps_s[0][0:64, :], lhsT=self.ones_r[0:1, 0:64],
                     rhs=dinv, start=True, stop=True, inc=(tP, 1))
                e.wait("vector", tP, tP.n)
                if qc > 0 and h < 2:
                    e.wait("vector", tP, st["aOmm%d" % (qc - 1)])
                e.op("vector", "tensor_copy", out=Dbc[:], in_=ps_s[0][0:64, :])
                e.op("vector", "tensor_tensor", out=headsN[pb2:pb2 + 64, dc, :],
                     in0=avp[0:64, :], in1=Dbc[:], op=ALU.mult, inc=(vP, 1))
                g_ps_s[0] = (vP, vP.n)
                g_ps_av[h % 2] = (vP, vP.n)
            v_hN = vP.n
            st["aQcons%d" % qc] = tP.n
            e.wait("tensor", vP, v_hN)
            if qc == 0:
                e.wait("tensor", sP, s_wo)
            for tl in range(4):
                tt = qc * 4 + tl
                b = tl % 2
                for c in range(NDC):
                    e.wait_g("tensor", g_ps_o[c])
                    for dcc in range(ND):
                        e.op("tensor", "matmul", out=ps_o[c][:],
                             lhsT=headsN[:, dcc, tl * 128:(tl + 1) * 128],
                             rhs=WT[:, dcc, c * 512:(c + 1) * 512],
                             start=(dcc == 0), stop=(dcc == ND - 1),
                             inc=(tP, 1) if dcc == ND - 1 else None)
                st["aOmm%d" % qc] = tP.n
                mm_o = tP.n
                e.wait("vector", tP, mm_o)
                e.wait_g("vector", g_x2[b])
                for c in range(NDC):
                    e.op("vector", "tensor_copy", out=x2_b[b][:, c * 512:(c + 1) * 512],
                         in_=ps_o[c][:], inc=(vP, 1) if c == NDC - 1 else None)
                st["ax2ev%d" % tt] = vP.n
                g_ps_o[0] = (vP, vP.n)
                g_ps_o[1] = (vP, vP.n)
                e.wait("gpsimd", vP, vP.n)
                if qc == 0 and tl == 0:
                    e.wait("gpsimd", x2pS, x2pS.n)
                e.dma("gpsimd", x2S, accum=True,
                      out=self.x2_scr.rearrange("(tt p) d -> p tt d", p=128)[:, tt, :],
                      in_=x2_b[b][:])
                g_x2[0] = g_x2[1] = (x2S, x2S.n)
        e.wait("gpsimd", x2S, x2S.n)
        self.env.barrier(self.bar)
        pes.close()


def make_inputs(cfg, full_inputs):
    X = np.ascontiguousarray(np.asarray(full_inputs["L0_attn_X"], dtype=np.float32))
    base = {
        "wq": np.ascontiguousarray(np.asarray(full_inputs["L0_attn_W_q"], np.float32)),
        "wk": np.ascontiguousarray(np.asarray(full_inputs["L0_attn_W_k"], np.float32)),
        "wv": np.ascontiguousarray(np.asarray(full_inputs["L0_attn_W_v"], np.float32)),
        "wo": np.ascontiguousarray(np.asarray(full_inputs["L0_attn_W_o"], np.float32)),
        "wg": np.ascontiguousarray(np.asarray(full_inputs["L0_ffn_w_gate"], np.float32)),
        "wu": np.ascontiguousarray(np.asarray(full_inputs["L0_ffn_w_up"], np.float32)),
        "wd": np.ascontiguousarray(np.asarray(full_inputs["L0_ffn_w_down"], np.float32)),
        "clipx": np.asarray(full_inputs["L0_ffn_clip_x"], np.float32).reshape(1, 1),
        "clipd": np.asarray(full_inputs["L0_ffn_clip_down"], np.float32).reshape(1, 1),
    }
    maps = []
    for c in range(cfg.n_cores):
        m = dict(base)
        m["x"] = np.ascontiguousarray(X[c])
        maps.append(m)
    return maps


_CFG = Cfg(T=2048, D=1024, H=16, I=2816, n_cores=8)
_MK_CACHE = {}


def _get_mk():
    if "mk" not in _MK_CACHE:
        _MK_CACHE["mk"] = MK(_CFG, with_attn=True)
    return _MK_CACHE["mk"]


def kernel(L0_attn_X, L0_attn_W_q, L0_attn_W_k, L0_attn_W_v, L0_attn_W_o,
           L0_ffn_clip_x, L0_ffn_clip_down, L0_ffn_w_gate, L0_ffn_w_up, L0_ffn_w_down):
    full = {
        "L0_attn_X": L0_attn_X, "L0_attn_W_q": L0_attn_W_q, "L0_attn_W_k": L0_attn_W_k,
        "L0_attn_W_v": L0_attn_W_v, "L0_attn_W_o": L0_attn_W_o,
        "L0_ffn_clip_x": L0_ffn_clip_x, "L0_ffn_clip_down": L0_ffn_clip_down,
        "L0_ffn_w_gate": L0_ffn_w_gate, "L0_ffn_w_up": L0_ffn_w_up,
        "L0_ffn_w_down": L0_ffn_w_down,
    }
    m = _get_mk()
    in_maps = make_inputs(_CFG, full)
    res = run_bass_kernel_spmd(m.nc, in_maps, core_ids=list(range(_CFG.n_cores)))
    return np.stack([np.asarray(res.results[i]["out"]) for i in range(_CFG.n_cores)], axis=0)
